# version 12
# baseline (speedup 1.0000x reference)
import numpy as np
import concourse.bass as bass
import concourse.mybir as mybir
from concourse.bass_utils import run_bass_kernel_spmd
from contextlib import ExitStack

F32 = mybir.dt.float32
BF16 = mybir.dt.bfloat16
AF = mybir.ActivationFunctionType
ALU = mybir.AluOpType
AX = mybir.AxisListType

NC = 8
TOK = 1024
NT = 8
D = 2048
S = 4096
H = 4
DH = 128
GROUPS = [[0, 1, 2, 3], [4, 5, 6, 7]]
LN_EPS = 1e-5
SHIFT = 22.0
E44 = float(np.exp(44.0))
EM44 = float(np.exp(-44.0))
SCALE_A = float(128.0 ** -0.5)
CSHIFT = 20.0

_TRUNC = 4
_DEBUG = ()

STREAMS = ("pe", "dve", "sp", "sc", "gp")
SEMS = ("pe", "dve", "sp", "act", "tq", "gq", "pool")
SEM2STREAM = {"pe": "pe", "dve": "dve", "sp": "sp", "act": "sc", "tq": "sc",
              "gq": "gp", "pool": "gp"}


def _rope_tables():
    inv = 1.0 / (10000.0 ** (np.arange(0, DH, 2, dtype=np.float64) / DH))
    t = np.arange(S, dtype=np.float64)
    fr = t[:, None] * inv[None, :]
    emb = np.concatenate([fr, fr], axis=-1)
    cos = np.ascontiguousarray(np.cos(emb).T.astype(np.float32))
    sin = np.sin(emb).T.astype(np.float32)
    sin[:64, :] *= -1.0
    return cos, np.ascontiguousarray(sin)


def _prep(I):
    x = np.asarray(I["L0_attn_a_hidden_states"], np.float32).reshape(8192, D)
    cos, sinp = _rope_tables()
    Ltri = np.ascontiguousarray(np.tril(np.ones((128, 128), np.float32)).T)
    half = np.ascontiguousarray((np.arange(128) <= 64).astype(np.float32)[:, None])
    ident = np.eye(128, dtype=np.float32)
    kl = np.arange(128)[:, None]
    ql = np.arange(512)[None, :]
    cmall = np.concatenate([(ql >= kl + dlt).astype(np.float32)
                            for dlt in (0, 128, 256, 384)], axis=1)

    A = lambda v: np.ascontiguousarray(np.asarray(v).astype(np.float32))
    consts = np.zeros((128, 16), np.float32)
    consts[:, 0] = LN_EPS
    consts[:, 1] = 1.0 + LN_EPS
    consts[:, 2] = 1e-6
    consts[:, 3] = 1.0
    consts[:, 4] = SHIFT
    consts[:, 5] = -CSHIFT
    consts[:, 6] = 1.0
    shared = dict(
        consts_in=consts,
        ones_row=np.ones((1, 512), np.float32),
        zeros_row=np.zeros((1, 512), np.float32),
        zeros128=np.zeros((128, 128), np.float32),
        vaug_ones=np.ones((128, 32 * 130), np.float32),
        cos_t=cos, sin_t=sinp, Ltri=Ltri, halfones=half, ident=ident, cmall=cmall,
        wf1=A(I["L0_attn_a_Wf1"]), wg1=A(I["L0_attn_a_Wg1"]),
        ln1w=A(np.broadcast_to(I["L0_ffn_ln1_w"], (128, D))),
        ln1b=A(np.broadcast_to(I["L0_ffn_ln1_b"], (128, D))),
        ln2w=A(np.broadcast_to(I["L0_ffn_ln2_w"], (128, D))),
        ln2b=A(np.broadcast_to(I["L0_ffn_ln2_b"], (128, D))),
        normw=A(np.broadcast_to(I["L0_attn_a_norm_w"], (128, DH))),
        wgate=A(np.asarray(I["L0_ffn_w_gate"]).reshape(16, 128, 4).transpose(1, 0, 2).reshape(128, 64)),
    )
    wob_full = np.asarray(I["L0_attn_b_out_w"]).T

    per_core = []
    for c in range(NC):
        g = c % 4
        cs = slice(512 * g, 512 * (g + 1))
        d = dict(shared)
        d["x"] = A(x[TOK * c:TOK * (c + 1)])
        d["wqa"] = A(I["L0_attn_a_Wq"][:, cs])
        d["wka"] = A(I["L0_attn_a_Wk"][:, cs])
        d["wva"] = A(I["L0_attn_a_Wv"][:, cs])
        d["wwa"] = A(I["L0_attn_a_Ww"][:, cs])
        for nm, key in (("cqa", "L0_attn_a_cq"), ("cka", "L0_attn_a_ck"),
                        ("cva", "L0_attn_a_cv")):
            d[nm] = A(np.asarray(I[key])[cs].reshape(4, 128, 4).transpose(1, 0, 2).reshape(128, 16))
        d["wf2s"] = A(I["L0_attn_a_Wf2"][:, cs])
        d["wg2s"] = A(I["L0_attn_a_Wg2"][:, cs])
        d["bg2s"] = A(np.broadcast_to(np.asarray(I["L0_attn_a_bg2"])[cs], (128, 512)))
        d["wbs"] = A(I["L0_attn_a_Wb"][:, 4 * g:4 * (g + 1)])
        d["woa_s"] = A(np.asarray(I["L0_attn_a_Wo"])[cs, :])
        d["wqb"] = A(np.asarray(I["L0_attn_b_q_w"])[cs, :].T)
        d["wkb"] = A(np.asarray(I["L0_attn_b_k_w"])[cs, :].T)
        d["wvb"] = A(np.asarray(I["L0_attn_b_v_w"])[cs, :].T)
        d["bqb"] = A(np.asarray(I["L0_attn_b_q_b"])[cs].reshape(4, 128).T)
        d["bkb"] = A(np.asarray(I["L0_attn_b_k_b"])[cs].reshape(4, 128).T)
        d["bvb"] = A(np.broadcast_to(np.asarray(I["L0_attn_b_v_b"])[cs], (128, 512)))
        d["wob_s"] = A(wob_full[cs, :])
        d["obb_bc"] = A(np.broadcast_to(np.asarray(I["L0_attn_b_out_b"]), (128, D)))
        d["wqf"] = A(I["L0_ffn_wq"][:, cs])
        d["wkf"] = A(I["L0_ffn_wk"][:, cs])
        d["wvf"] = A(I["L0_ffn_wv"][:, cs])
        d["bqf"] = A(np.asarray(I["L0_ffn_bq"])[cs].reshape(4, 128).T)
        d["bkf"] = A(np.asarray(I["L0_ffn_bk"])[cs].reshape(4, 128).T)
        d["bvf"] = A(np.broadcast_to(np.asarray(I["L0_ffn_bv"])[cs], (128, 512)))
        d["wof_s"] = A(np.asarray(I["L0_ffn_wo"])[cs, :])
        d["bof_bc"] = A(np.broadcast_to(np.asarray(I["L0_ffn_bo"]), (128, D)))
        d["we1e"] = A(np.asarray(I["L0_ffn_we1"])[g])
        d["we2e"] = A(np.asarray(I["L0_ffn_we2"])[g])
        d["be1e"] = A(np.asarray(I["L0_ffn_be1"])[g].reshape(16, 128).T)
        d["be2e"] = A(np.asarray(I["L0_ffn_be2"])[g][None, :])
        oh = np.zeros((16, 4), np.float32)
        for r in range(4):
            oh[4 * r + g, r] = 1.0
        d["onehots"] = oh
        per_core.append(d)
    return per_core


class Prog:

    def __init__(self, nc):
        self.nc = nc
        self.seq = {s: [] for s in STREAMS}
        self.cnt = {s: 0 for s in SEMS}
        self.stk = ExitStack()
        self.sem = {s: self.stk.enter_context(nc.semaphore(s + "_s")) for s in SEMS}
        self.ccs = []

    def _depmap(self, deps):
        m = {}
        for dd in deps:
            if dd is None:
                continue
            s, v = dd
            m[s] = max(m.get(s, 0), v)
        return m

    def op(self, sem_name, emit, deps=(), inc=1):
        stream = SEM2STREAM[sem_name]
        deps = list(deps)
        if sem_name in ("dve", "act", "pool") and self.cnt[sem_name] > 0:
            deps.append((sem_name, self.cnt[sem_name]))
        self.cnt[sem_name] += inc
        self.seq[stream].append((emit, self._depmap(deps), (sem_name, inc)))
        return (sem_name, self.cnt[sem_name])

    def pe(self, emit, deps=()):
        return self.op("pe", emit, deps)

    def dve(self, emit, deps=()):
        return self.op("dve", emit, deps)

    def act(self, emit, deps=()):
        return self.op("act", emit, deps)

    def pool(self, emit, deps=()):
        return self.op("pool", emit, deps)

    def dma(self, q, dst, src, deps=(), **kw):
        return self.op(q, lambda e: e.dma_start(dst, src, **kw), deps, inc=16)

    def tload(self, dst, src, deps=()):
        return self.op("tq", lambda e: e.dma_start(dst, src, transpose=True), deps, inc=16)

    def coll(self, kind, alu, ins, outs, deps=()):
        h = self.stk.enter_context(self.nc.semaphore(f"cc{len(self.ccs)}_s"))
        self.ccs.append(h)
        idx = len(self.ccs) - 1

        def emit(e):
            return e.collective_compute(kind, alu, replica_groups=GROUPS,
                                        ins=ins, outs=outs)
        self.seq["gp"].append((emit, self._depmap(deps), ("cc%d" % idx, 1)))
        return ("cc%d" % idx, 1)

    def now(self, sem_name):
        return (sem_name, self.cnt[sem_name])

    def emit(self, block):
        def runner(stream):
            def run(eng):
                seen = {}
                for emit, deps, inc in self.seq[stream]:
                    for s, v in deps.items():
                        if seen.get(s, -1) >= v:
                            continue
                        seen[s] = v
                        h = self.ccs[int(s[2:])] if s.startswith("cc") else self.sem[s]
                        eng.wait_ge(h, v)
                    ins = emit(eng)
                    if ins is not None and inc[1]:
                        s, k = inc
                        h = self.ccs[int(s[2:])] if s.startswith("cc") else self.sem[s]
                        ins.then_inc(h, k)
            return run
        block.tensor(runner("pe"))
        block.vector(runner("dve"))
        block.sync(runner("sp"))
        block.scalar(runner("sc"))
        block.gpsimd(runner("gp"))


def build():
    nc = bass.Bass("TRN2", target_bir_lowering=False)
    dt = nc.dram_tensor
    x_in = dt("x", [TOK, D], F32, kind="ExternalInput").ap()
    y_out = dt("y", [TOK, D], F32, kind="ExternalOutput").ap()

    IN = {}
    for nm, shp in [
        ("cos_t", [128, S]), ("sin_t", [128, S]), ("Ltri", [128, 128]),
        ("halfones", [128, 1]), ("ident", [128, 128]), ("cmall", [128, 2048]),
        ("wf1", [D, 128]), ("wg1", [D, 128]),
        ("ln1w", [128, D]), ("ln1b", [128, D]), ("ln2w", [128, D]), ("ln2b", [128, D]),
        ("normw", [128, DH]), ("wgate", [128, 64]),
        ("wqa", [D, 512]), ("wka", [D, 512]), ("wva", [D, 512]), ("wwa", [D, 512]),
        ("cqa", [128, 16]), ("cka", [128, 16]), ("cva", [128, 16]),
        ("wf2s", [128, 512]), ("wg2s", [128, 512]), ("bg2s", [128, 512]), ("wbs", [D, 4]),
        ("woa_s", [512, D]),
        ("wqb", [D, 512]), ("wkb", [D, 512]), ("wvb", [D, 512]),
        ("bqb", [128, 4]), ("bkb", [128, 4]), ("bvb", [128, 512]),
        ("wob_s", [512, D]), ("obb_bc", [128, D]),
        ("wqf", [D, 512]), ("wkf", [D, 512]), ("wvf", [D, 512]),
        ("bqf", [128, 4]), ("bkf", [128, 4]), ("bvf", [128, 512]),
        ("wof_s", [512, D]), ("bof_bc", [128, D]),
        ("we1e", [D, D]), ("we2e", [D, D]), ("be1e", [128, 16]), ("be2e", [1, D]),
        ("onehots", [16, 4]),
        ("consts_in", [128, 16]), ("ones_row", [1, 512]), ("zeros_row", [1, 512]),
        ("zeros128", [128, 128]), ("vaug_ones", [128, 32 * 130]),
    ]:
        IN[nm] = dt(nm, shp, F32, kind="ExternalInput").ap()

    x_dram = dt("x_dram", [TOK, D], F32)
    ag_in = dt("ag_in", [TOK, D], BF16)
    ag_out = dt("ag_out", [4 * TOK, D], BF16)
    qT_d = dt("qT_d", [512, S], BF16)
    kT_d = dt("kT_d", [512, S], BF16)
    vT_d = dt("vT_d", [512, S], BF16)
    wT_d = dt("wT_d", [512, S], BF16)
    zT_d = dt("zT_d", [512, S], BF16)
    v_tm = dt("v_tm", [S, 512], BF16)
    qcT_d = dt("qcT_d", [512, S], BF16)
    kcT_d = dt("kcT_d", [512, S], BF16)
    vcT_d = dt("vcT_d", [512, S], BF16)
    wcT_d = dt("wcT_d", [512, S], BF16)
    g_d = dt("g_d", [512, S], F32)
    beta_d = dt("beta_d", [4, S], BF16)
    gate_d = dt("gate_d", [S, 512], BF16)
    hm_d = dt("hm_d", [D, S], BF16)
    rs_in = dt("rs_in", [S, D], F32)
    rs_out = dt("rs_out", [TOK, D], F32)
    rs_in_c = [dt(f"rs_in_c{n}", [S, 512], F32) for n in range(4)]
    rs_out_c = [dt(f"rs_out_c{n}", [TOK, 512], F32) for n in range(4)]
    combT_d = dt("combT_d", [4, TOK], F32)
    oT_dbg = dt("oT_dbg", [512, S], BF16)
    sc_dbg = dt("sc_dbg", [128, 32 * 128], F32)
    combT_g = dt("combT_g", [16, TOK], F32)

    stk = ExitStack()
    sb = lambda name, shp, dtp: stk.enter_context(nc.sbuf_tensor(name, shp, dtp))
    P = [stk.enter_context(nc.psum_tensor(f"P{i}", [128, 512], F32)) for i in range(7)]
    PT = stk.enter_context(nc.psum_tensor("PT", [128, 1024], BF16))

    idf = sb("idf", [128, 128], F32)
    idb = sb("idb", [128, 128], BF16)
    ltri = sb("ltri", [128, 128], F32)
    mst = sb("mst", [128, 128], BF16)
    hones = sb("hones", [128, 1], F32)
    cosb = sb("cosb", [128, S], BF16)
    sinb = sb("sinb", [128, S], BF16)
    cmb = sb("cmb", [128, 2048], BF16)
    nwb = sb("nwb", [128, DH], BF16)
    ones_cb = sb("ones_cb", [128, 1], BF16)
    ones_cf = sb("ones_cf", [128, 1], F32)
    ones_r = sb("ones_r", [1, 512], F32)
    ones_rb = sb("ones_rb", [1, 128], BF16)
    ln1w_b = sb("ln1w_b", [128, D], BF16)
    ln1b_b = sb("ln1b_b", [128, D], BF16)
    cq_s = sb("cq_s", [128, 16], F32)
    ck_s = sb("ck_s", [128, 16], F32)
    cv_s = sb("cv_s", [128, 16], F32)
    be1_s = sb("be1_s", [128, 16], F32)
    wgate_s = sb("wgate_s", [128, 64], F32)
    onehot_s = sb("onehot_s", [16, 4], F32)
    bias_c = {nm: sb(nm + "_c", [128, 4], F32) for nm in ("bqb", "bkb", "bqf", "bkf")}
    bias_b = {nm: sb(nm + "_b", [128, 512], F32) for nm in ("bvb", "bvf", "bg2s")}
    rbias = {nm: sb(nm + "_rb", [128, D], F32) for nm in ("obb_bc", "bof_bc")}
    r0_t = sb("r0_t", [128, NT], F32)
    nm0_t = sb("nm0_t", [128, NT], F32)
    rr_t = sb("rr_t", [128, NT], F32)
    nmr_t = sb("nmr_t", [128, NT], F32)
    sqd = sb("sqd", [128, D], BF16)
    cst = sb("cst", [128, 16], F32)
    zcol = sb("zcol", [128, 128], F32)
    st1 = sb("st1", [128, NT], F32)
    st2 = sb("st2", [128, NT], F32)
    st3 = sb("st3", [128, NT], F32)

    blockstk = ExitStack()
    block = blockstk.enter_context(nc.Block())
    p = Prog(nc)

    for dst, src in [(idf[:, :], IN["ident"]), (ltri[:, :], IN["Ltri"]),
                     (hones[:, :], IN["halfones"]), (cq_s[:, :], IN["cqa"]),
                     (ck_s[:, :], IN["cka"]), (cv_s[:, :], IN["cva"]),
                     (be1_s[:, :], IN["be1e"]), (wgate_s[:, :], IN["wgate"]),
                     (onehot_s[:, :], IN["onehots"])]:
        p.dma("sp", dst, src)
    for nm in bias_c:
        p.dma("sp", bias_c[nm][:, :], IN[nm])
    for nm in bias_b:
        p.dma("gq", bias_b[nm][:, :], IN[nm])
    for nm in rbias:
        p.dma("gq", rbias[nm][:, :], IN[nm])
    for dst, src in [(idb[:, :], IN["ident"]), (mst[:, :], IN["Ltri"]),
                     (cosb[:, :], IN["cos_t"]), (sinb[:, :], IN["sin_t"]),
                     (cmb[:, :], IN["cmall"]), (nwb[:, :], IN["normw"]),
                     (ln1w_b[:, :], IN["ln1w"]), (ln1b_b[:, :], IN["ln1b"])]:
        p.dma("gq", dst, src)
    p.dma("sp", cst[:, :], IN["consts_in"])
    p.dma("sp", ones_r[:, :], IN["ones_row"])
    p.dma("sp", zcol[:, :], IN["zeros128"])
    p.dma("gq", ones_cb[:, :], IN["consts_in"][:, 6:7], allow_slow_non_contiguous=True)
    p.dma("sp", ones_cf[:, :], IN["consts_in"][:, 6:7], allow_slow_non_contiguous=True)
    SET = p.dma("gq", ones_rb[:, :], IN["ones_row"][0:1, 0:128])

    uid = [0]

    def fresh(prefix):
        uid[0] += 1
        return f"{prefix}_{uid[0]}"

    def cast_w(dst, src_ap, cols, nk=16, nsplit=4, deps=()):
        kper = nk // nsplit
        t = None
        for j in range(nsplit):
            t = p.dma(
                "gq",
                dst[:, cols * kper * j:cols * kper * (j + 1)].rearrange("p (k c) -> p k c", c=cols),
                src_ap[128 * kper * j:128 * kper * (j + 1), :].rearrange("(k p) c -> p k c", p=128),
                deps=deps if j == 0 else ())
        return t

    def ln_phase(x_src, mode, ag_free_dep, entry=None):
        with ExitStack() as ph:
            xs = [ph.enter_context(nc.sbuf_tensor(fresh("lnx"), [128, D], F32))
                  for _ in range(NT)]
            lnb = ph.enter_context(nc.sbuf_tensor(fresh("lnb"), [128, D], BF16))
            ld, vt, at, ok = [], [], [], []
            for i in range(NT):
                ld.append(p.dma("sp", xs[i][:, :], x_src[128 * i:128 * (i + 1), :],
                                deps=[entry] if i == 0 else ()))
            for i in range(NT):
                vt.append(p.dve(lambda v, i=i: v.reduce_sum(st1[:, i:i + 1], xs[i][:, :], AX.X),
                                deps=[ld[i]]))
                at.append(p.act(lambda s, i=i: s.activation(
                    sqd[:, :], xs[i][:, :], AF.Square, accum_out=st2[:, i:i + 1]),
                    deps=[ld[i]]))
            for i in range(NT):
                p.dve(lambda v, i=i: v.tensor_scalar(st1[:, i:i + 1], st1[:, i:i + 1],
                                                     1.0 / D, None, ALU.mult), deps=[vt[i]])
                p.dve(lambda v, i=i: v.tensor_scalar(st2[:, i:i + 1], st2[:, i:i + 1],
                                                     1.0 / D, None, ALU.mult), deps=[at[i]])
                p.dve(lambda v, i=i: v.scalar_tensor_tensor(
                    st2[:, i:i + 1], st1[:, i:i + 1], st1[:, i:i + 1], st2[:, i:i + 1],
                    ALU.mult, ALU.subtract))
                t = p.dve(lambda v, i=i: v.tensor_scalar(st2[:, i:i + 1], st2[:, i:i + 1],
                                                         -1.0, None, ALU.mult))
                t = p.act(lambda s, i=i: s.activation(st3[:, i:i + 1], st2[:, i:i + 1],
                                                      AF.Ln, bias=cst[:, 0:1]), deps=[t])
                t = p.act(lambda s, i=i: s.activation(r0_t[:, i:i + 1], st3[:, i:i + 1],
                                                      AF.Exp, scale=-0.5))
                t = p.dve(lambda v, i=i: v.tensor_scalar(
                    nm0_t[:, i:i + 1], st1[:, i:i + 1], r0_t[:, i:i + 1], -1.0,
                    ALU.mult, ALU.mult), deps=[t])
                if mode == "ffn":
                    t = p.dve(lambda v, i=i: v.tensor_scalar(
                        st3[:, i:i + 1], r0_t[:, i:i + 1], r0_t[:, i:i + 1], -LN_EPS,
                        ALU.mult, ALU.mult))
                    t = p.dve(lambda v, i=i: v.tensor_scalar(st3[:, i:i + 1], st3[:, i:i + 1],
                                                              1.0 + LN_EPS, None, ALU.add), deps=[t])
                    t = p.act(lambda s, i=i: s.activation(st3[:, i:i + 1], st3[:, i:i + 1],
                                                          AF.Ln), deps=[t])
                    t = p.act(lambda s, i=i: s.activation(st3[:, i:i + 1], st3[:, i:i + 1],
                                                          AF.Exp, scale=-0.5))
                    t = p.dve(lambda v, i=i: v.tensor_tensor(
                        rr_t[:, i:i + 1], r0_t[:, i:i + 1], st3[:, i:i + 1], ALU.mult), deps=[t])
                    t = p.dve(lambda v, i=i: v.tensor_scalar(
                        nmr_t[:, i:i + 1], st1[:, i:i + 1], rr_t[:, i:i + 1], -1.0,
                        ALU.mult, ALU.mult))
                ok.append(t)
            sto = ag_free_dep
            for i in range(NT):
                if mode == "ffn":
                    t = p.dve(lambda v, i=i: v.tensor_scalar(
                        lnb[:, :], xs[i][:, :], rr_t[:, i:i + 1], nmr_t[:, i:i + 1],
                        ALU.mult, ALU.add), deps=[ok[i], sto])
                    t = p.dve(lambda v: v.tensor_tensor(lnb[:, :], lnb[:, :], ln1w_b[:, :], ALU.mult))
                    t = p.dve(lambda v: v.tensor_tensor(lnb[:, :], lnb[:, :], ln1b_b[:, :], ALU.add))
                else:
                    t = p.dve(lambda v, i=i: v.tensor_scalar(
                        lnb[:, :], xs[i][:, :], r0_t[:, i:i + 1], nm0_t[:, i:i + 1],
                        ALU.mult, ALU.add), deps=[ok[i], sto])
                sto = p.dma("sp", ag_in.ap()[128 * i:128 * (i + 1), :], lnb[:, :], deps=[t])
            return sto, ld[-1]

    def load_rhsT(rhsT, r, h, ccdep, extra=()):
        t = None
        base = TOK * r + 512 * h
        for k in range(16):
            t = p.tload(rhsT[:, 512 * k:512 * (k + 1)],
                        ag_out.ap()[base:base + 512, 128 * k:128 * (k + 1)],
                        deps=[ccdep] + list(extra))
        return t

    def proj_qkv(wq_ap, wk_ap, wv_ap, bqn, bkn, bvn, ccdep, seed):
        with ExitStack() as ph:
            psb = lambda shp, d_: ph.enter_context(nc.sbuf_tensor(fresh("pq"), shp, d_))
            wsb = {k: psb([128, 8192], BF16) for k in "qkv"}
            rhsT = psb([128, 8192], BF16)
            q0 = psb([128, 512], BF16)
            rot = psb([128, 512], BF16)
            t1b = psb([128, 512], BF16)
            stg = [psb([128, 512], BF16) for _ in range(2)]
            wt = {"q": cast_w(wsb["q"], wq_ap, 512, deps=[seed]),
                  "k": cast_w(wsb["k"], wk_ap, 512),
                  "v": cast_w(wsb["v"], wv_ap, 512)}
            bcol = {"q": bias_c[bqn], "k": bias_c[bkn]}
            bvbc = bias_b[bvn]
            prev = [None, None]
            laststore = seed
            last_rot = None
            pi = 0
            for r in range(4):
                for hh in range(2):
                    rt = load_rhsT(rhsT, r, hh, ccdep, extra=[laststore])
                    cols = slice(TOK * r + 512 * hh, TOK * r + 512 * (hh + 1))
                    for wkey in ("q", "k"):
                        outd = qT_d if wkey == "q" else kT_d
                        for m in range(4):
                            slot = pi % 2
                            bank = P[slot]
                            pi += 1

                            def mmf(t, wkey=wkey, m=m, bank=bank):
                                ins = None
                                for k in range(16):
                                    ins = t.matmul(bank[:, :],
                                                   wsb[wkey][:, 512 * k + 128 * m:512 * k + 128 * (m + 1)],
                                                   rhsT[:, 512 * k:512 * (k + 1)],
                                                   start=(k == 0), stop=(k == 15))
                                return ins
                            mt = p.pe(mmf, deps=[rt, wt[wkey], prev[slot]])
                            t = p.dve(lambda v, bank=bank, wkey=wkey, m=m: v.tensor_scalar(
                                q0[:, :], bank[:, :], bcol[wkey][:, m:m + 1], None, ALU.add),
                                deps=[mt, last_rot])
                            prev[slot] = t
                            t2 = p.pool(lambda g: g.tensor_copy(rot[0:64, :], q0[64:128, :]),
                                        deps=[t])
                            t2 = p.pool(lambda g: g.tensor_copy(rot[64:128, :], q0[0:64, :]))
                            last_rot = t2
                            qsc = SCALE_A if wkey == "q" else 1.0
                            t3 = p.dve(lambda v, cols=cols, qsc=qsc: v.scalar_tensor_tensor(
                                t1b[:, :], q0[:, :], qsc, cosb[:, cols], ALU.mult, ALU.mult),
                                deps=[t])
                            sg = stg[slot]
                            t4 = p.dve(lambda v, cols=cols, sg=sg, qsc=qsc: v.scalar_tensor_tensor(
                                sg[:, :], rot[:, :], qsc, sinb[:, cols], ALU.mult, ALU.mult),
                                deps=[t2, t3, laststore])
                            t4 = p.dve(lambda v, sg=sg: v.tensor_tensor(sg[:, :], sg[:, :],
                                                                        t1b[:, :], ALU.add))
                            laststore = p.dma("sp", outd.ap()[128 * m:128 * (m + 1), cols],
                                              sg[:, :], deps=[t4])
                    for j in range(4):
                        slot = pi % 2
                        bank = P[slot]
                        pi += 1

                        def mmv(t, j=j, bank=bank):
                            ins = None
                            for k in range(16):
                                ins = t.matmul(bank[:, :],
                                               rhsT[:, 512 * k + 128 * j:512 * k + 128 * (j + 1)],
                                               wsb["v"][:, 512 * k:512 * (k + 1)],
                                               start=(k == 0), stop=(k == 15))
                            return ins
                        mt = p.pe(mmv, deps=[rt, wt["v"], prev[slot]])
                        sg = stg[slot]
                        t = p.dve(lambda v, bank=bank, sg=sg: v.tensor_tensor(
                            sg[:, :], bank[:, :], bvbc[:, :], ALU.add),
                            deps=[mt, laststore])
                        prev[slot] = t
                        row0 = TOK * r + 512 * hh + 128 * j
                        laststore = p.dma("sp", v_tm.ap()[row0:row0 + 128, :],
                                          sg[:, :], deps=[t])
            return laststore

    def attn_inner(causal, oT, proj_dep):
        with ExitStack() as ph:
            psb = lambda shp, d_: ph.enter_context(nc.sbuf_tensor(fresh("ai"), shp, d_))
            qTh = psb([128, S], BF16)
            kTh = psb([128, S], BF16)
            vaug = psb([128, 32 * 130], BF16)
            pt = [psb([128, 512], BF16) for _ in range(3)]
            ostg = psb([128, 128], BF16)
            recs = psb([128, 1], F32)
            otok = None
            vload = p.dma("gq", vaug[:, :], IN["vaug_ones"], deps=[proj_dep])
            prev_bh = [proj_dep]
            for bh in range(H):
                rows = slice(128 * bh, 128 * (bh + 1))
                lq = p.dma("sp", qTh[:, :], qT_d.ap()[rows, :], deps=prev_bh)
                lk = p.dma("sp", kTh[:, :], kT_d.ap()[rows, :])
                lv = p.dma("sp", vaug[:, :].rearrange("p (k j) -> p k j", j=130)[:, :, 0:128],
                           v_tm.ap()[:, 128 * bh:128 * (bh + 1)].rearrange("(k p) j -> p k j", p=128),
                           deps=[vload] + prev_bh)
                exp_tok = [None, None]
                ptok = [None, None, None]
                ob = [P[3], P[4], P[5], P[6]]
                oev = [None, None, None, None]
                for qc in range(8):
                    nkt = 4 * (qc + 1) if causal else 32
                    acc = [None] * 4
                    for kt in range(nkt):
                        slot = kt % 2
                        bank = P[slot]

                        def smm(t, kt=kt, qc=qc, bank=bank):
                            return t.matmul(bank[:, :], kTh[:, 128 * kt:128 * (kt + 1)],
                                            qTh[:, 512 * qc:512 * (qc + 1)],
                                            start=True, stop=True)
                        st_ = p.pe(smm, deps=[lq, lk, exp_tok[slot]])
                        pti = kt % 3

                        def ex(s_, bank=bank, pti=pti):
                            return s_.activation(pt[pti][:, :], bank[:, :], AF.Exp, bias=cst[:, 5:6])
                        et = p.act(ex, deps=[st_, ptok[pti]])
                        exp_tok[slot] = et
                        ready = et
                        if causal and kt >= 4 * qc:
                            mi = (128 * kt - 512 * qc) // 128

                            def mk(v, pti=pti, mi=mi):
                                return v.tensor_tensor(pt[pti][:, :], pt[pti][:, :],
                                                       cmb[:, 512 * mi:512 * (mi + 1)], ALU.mult)
                            ready = p.dve(mk, deps=[et])
                        last = None
                        for qt in range(4):
                            def pv(t, pti=pti, qt=qt, kt=kt, first=(acc[qt] is None)):
                                return t.matmul(ob[qt][:, 0:129],
                                                pt[pti][:, 128 * qt:128 * (qt + 1)],
                                                vaug[:, 130 * kt:130 * kt + 129],
                                                start=first, stop=False)
                            last = p.pe(pv, deps=[ready, lv, acc[qt] or oev[qt]])
                            acc[qt] = last
                        ptok[pti] = last
                    for qt in range(4):
                        t = p.dve(lambda v, qt=qt: v.reciprocal(recs[:, 0:1], ob[qt][:, 128:129]),
                                  deps=[acc[qt], otok])
                        t = p.dve(lambda v, qt=qt: v.tensor_scalar(
                            ostg[:, :], ob[qt][:, 0:128], recs[:, 0:1], None, ALU.mult))
                        tt = p.pe(lambda t_: t_.transpose(PT[:, 0:128], ostg[:, :], idb[:, :]),
                                  deps=[t, otok])
                        otok = p.dve(lambda v, bh=bh, qc=qc, qt=qt: v.tensor_copy(
                            oT[bh][:, 512 * qc + 128 * qt:512 * qc + 128 * (qt + 1)],
                            PT[:, 0:128]), deps=[tt])
                        oev[qt] = otok
                prev_bh = [otok]
            return otok

    def out_proj_rs(oTbig, wo_nm, dep, rs_free_ccs, rsout_free):
        with ExitStack() as ph:
            psb = lambda shp, d_: ph.enter_context(nc.sbuf_tensor(fresh("op"), shp, d_))
            wo_sb = psb([128, 4 * D], BF16)
            stg = [psb([128, 512], F32) for _ in range(2)]
            wt = cast_w(wo_sb, IN[wo_nm], D, nk=4, nsplit=1, deps=[dep])
            prev = [None, None]
            ccs = []
            for n in range(4):
                sto = rs_free_ccs[n]
                for tt in range(32):
                    slot = (n * 32 + tt) % 2
                    bank = P[slot]

                    def mmf(t, tt=tt, n=n, bank=bank):
                        ins = None
                        for k in range(4):
                            ins = t.matmul(bank[:, :],
                                           oTbig[:, S * k + 128 * tt:S * k + 128 * (tt + 1)],
                                           wo_sb[:, D * k + 512 * n:D * k + 512 * (n + 1)],
                                           start=(k == 0), stop=(k == 3))
                        return ins
                    mt = p.pe(mmf, deps=[dep, wt, prev[slot]])
                    sg = stg[slot]
                    t = p.dve(lambda v, bank=bank, sg=sg: v.tensor_copy(sg[:, :], bank[:, :]),
                              deps=[mt, sto])
                    prev[slot] = t
                    sto = p.dma("sp", rs_in_c[n].ap()[128 * tt:128 * (tt + 1), :],
                                sg[:, :], deps=[t])
                ccs.append(p.coll("ReduceScatter", ALU.add, [rs_in_c[n].ap().opt()],
                                  [rs_out_c[n].ap().opt()],
                                  deps=[sto, rsout_free[n]]))
            return ccs

    def residual(x_src, cc, mode, h3buf=None, final=False, bias=None):
        chunked = isinstance(cc, list)
        with ExitStack() as ph:
            psb = lambda shp, d_: ph.enter_context(nc.sbuf_tensor(fresh("rs"), shp, d_))
            xts = [psb([128, D], F32) for _ in range(2)]
            ats = [psb([128, D], F32) for _ in range(2)]
            dones = [None, None]
            cfree = [None] * 4
            last = None
            for i in range(NT):
                s = i % 2
                xt, at = xts[s], ats[s]
                lx = p.dma("sp", xt[:, :], x_src[128 * i:128 * (i + 1), :], deps=[dones[s]])
                if chunked:
                    for n in range(4):
                        la = p.dma("sp", at[:, 512 * n:512 * (n + 1)],
                                   rs_out_c[n].ap()[128 * i:128 * (i + 1), :],
                                   deps=[cc[n], dones[s]])
                        cfree[n] = la
                else:
                    la = p.dma("sp", at[:, :], rs_out.ap()[128 * i:128 * (i + 1), :],
                               deps=[cc, dones[s]])
                if mode == "plain":
                    t = p.dve(lambda v, xt=xt, at=at: v.tensor_tensor(
                        xt[:, :], xt[:, :], at[:, :], ALU.add), deps=[lx, la])
                    if bias is not None:
                        t = p.dve(lambda v, xt=xt: v.tensor_tensor(xt[:, :], xt[:, :], bias[:, :],
                                                                   ALU.add))
                else:
                    t = p.dve(lambda v, i=i, xt=xt, at=at: v.tensor_scalar(
                        h3buf[i][:, :], xt[:, :], r0_t[:, i:i + 1], nm0_t[:, i:i + 1],
                        ALU.mult, ALU.add), deps=[lx, la])
                    t = p.dve(lambda v, i=i, at=at: v.tensor_tensor(
                        h3buf[i][:, :], h3buf[i][:, :], at[:, :], ALU.add))
                    if bias is not None:
                        t = p.dve(lambda v, i=i: v.tensor_tensor(
                            h3buf[i][:, :], h3buf[i][:, :], bias[:, :], ALU.add))
                    t = p.dve(lambda v, i=i, xt=xt: v.tensor_tensor(
                        xt[:, :], xt[:, :], h3buf[i][:, :], ALU.add))
                last = p.dma("sp", x_dram.ap()[128 * i:128 * (i + 1), :], xt[:, :], deps=[t])
                if final:
                    last = p.dma("sp", y_out[128 * i:128 * (i + 1), :], xt[:, :])
                dones[s] = last
            return last, (cfree if chunked else last)

    def proj_a(ccdep, join):
        with ExitStack() as ph:
            psb = lambda shp, d_: ph.enter_context(nc.sbuf_tensor(fresh("pa"), shp, d_))
            w4 = {nm: psb([128, 8192], BF16) for nm in ("q", "k", "v", "w")}
            wf1_sb = psb([128, 2048], BF16)
            wg1_sb = psb([128, 2048], BF16)
            wf2_sb = psb([128, 512], BF16)
            wg2_sb = psb([128, 512], BF16)
            wb_sb = psb([128, 64], BF16)
            rhsT = psb([128, 8192], BF16)
            z1_bf = psb([128, S], BF16)
            g1_bf = psb([128, S], BF16)
            stg = [psb([128, 512], BF16) for _ in range(2)]
            gbs2 = [psb([128, 512], F32) for _ in range(2)]
            wt = {}
            wt["q"] = cast_w(w4["q"], IN["wqa"], 512, deps=[join])
            wt["k"] = cast_w(w4["k"], IN["wka"], 512)
            wt["v"] = cast_w(w4["v"], IN["wva"], 512)
            wt["w"] = cast_w(w4["w"], IN["wwa"], 512)
            wt["f1"] = cast_w(wf1_sb, IN["wf1"], 128)
            wt["g1"] = cast_w(wg1_sb, IN["wg1"], 128)
            wt["b"] = cast_w(wb_sb, IN["wbs"], 4)
            p.dma("gq", wf2_sb[:, :], IN["wf2s"])
            wt["small"] = p.dma("gq", wg2_sb[:, :], IN["wg2s"])
            prev = [None, None]
            sto = join
            zt = None
            pi = 0
            for r in range(4):
                for hh in range(2):
                    rt = load_rhsT(rhsT, r, hh, ccdep, extra=[sto])
                    cols = slice(TOK * r + 512 * hh, TOK * r + 512 * (hh + 1))
                    for wkey, outd in (("q", qT_d), ("k", kT_d), ("v", vT_d), ("w", wT_d)):
                        for m in range(4):
                            slot = pi % 2
                            bank = P[slot]
                            pi += 1

                            def mmf(t, wkey=wkey, m=m, bank=bank):
                                ins = None
                                for k in range(16):
                                    ins = t.matmul(bank[:, :],
                                                   w4[wkey][:, 512 * k + 128 * m:512 * k + 128 * (m + 1)],
                                                   rhsT[:, 512 * k:512 * (k + 1)],
                                                   start=(k == 0), stop=(k == 15))
                                return ins
                            mt = p.pe(mmf, deps=[rt, wt[wkey], prev[slot]])
                            sg = stg[slot]
                            t = p.dve(lambda v, bank=bank, sg=sg: v.tensor_copy(sg[:, :], bank[:, :]),
                                      deps=[mt, sto])
                            prev[slot] = t
                            sto = p.dma("sp", outd.ap()[128 * m:128 * (m + 1), cols],
                                        sg[:, :], deps=[t])
                    for wsb1, dst in ((wf1_sb, z1_bf), (wg1_sb, g1_bf)):
                        slot = pi % 2
                        bank = P[slot]
                        pi += 1

                        def mmf2(t, wsb1=wsb1, bank=bank):
                            ins = None
                            for k in range(16):
                                ins = t.matmul(bank[:, :], wsb1[:, 128 * k:128 * (k + 1)],
                                               rhsT[:, 512 * k:512 * (k + 1)],
                                               start=(k == 0), stop=(k == 15))
                            return ins
                        mt = p.pe(mmf2, deps=[rt, wt["f1"], wt["g1"], prev[slot]])
                        zt = p.dve(lambda v, bank=bank, dst=dst, cols=cols: v.tensor_copy(
                            dst[:, cols], bank[:, :]), deps=[mt])
                        prev[slot] = zt
                    slot = pi % 2
                    bank = P[slot]
                    pi += 1

                    def mmb(t, bank=bank):
                        ins = None
                        for k in range(16):
                            ins = t.matmul(bank[0:4, :], wb_sb[:, 4 * k:4 * (k + 1)],
                                           rhsT[:, 512 * k:512 * (k + 1)],
                                           start=(k == 0), stop=(k == 15))
                        return ins
                    mt = p.pe(mmb, deps=[rt, wt["b"], prev[slot]])
                    sg = stg[slot]
                    t = p.act(lambda s_, bank=bank, sg=sg: s_.activation(
                        sg[0:4, :], bank[0:4, :], AF.Sigmoid), deps=[mt, sto])
                    prev[slot] = t
                    sto = p.dma("sp", beta_d.ap()[0:4, cols], sg[0:4, :], deps=[t])
            for m in range(4):
                for n in range(8):
                    slot = n % 2
                    bank = P[slot]

                    def mmz(t, m=m, n=n, bank=bank):
                        return t.matmul(bank[:, :], wf2_sb[:, 128 * m:128 * (m + 1)],
                                        z1_bf[:, 512 * n:512 * (n + 1)], start=True, stop=True)
                    mt = p.pe(mmz, deps=[zt, wt["small"], prev[slot]])
                    sg = stg[slot]
                    t = p.dve(lambda v, bank=bank, sg=sg: v.tensor_copy(sg[:, :], bank[:, :]),
                              deps=[mt, sto])
                    prev[slot] = t
                    sto = p.dma("sp", zT_d.ap()[128 * m:128 * (m + 1), 512 * n:512 * (n + 1)],
                                sg[:, :], deps=[t])
            for tt in range(32):
                slot = tt % 2
                bank = P[slot]

                def mmg(t, tt=tt, bank=bank):
                    return t.matmul(bank[:, :], g1_bf[:, 128 * tt:128 * (tt + 1)], wg2_sb[:, :],
                                    start=True, stop=True)
                mt = p.pe(mmg, deps=[zt, wt["small"], prev[slot]])
                sg = stg[slot]
                gb = gbs2[slot]
                t = p.dve(lambda v, bank=bank, gb=gb: v.tensor_tensor(
                    gb[:, :], bank[:, :], bias_b["bg2s"][:, :], ALU.add), deps=[mt])
                t = p.act(lambda s_, sg=sg, gb=gb: s_.activation(
                    sg[:, :], gb[:, :], AF.Sigmoid), deps=[t, sto])
                prev[slot] = t
                sto = p.dma("sp", gate_d.ap()[128 * tt:128 * (tt + 1), :], sg[:, :], deps=[t])
            return sto

    def conv_phase(proj_done):
        with ExitStack() as ph:
            psb = lambda shp, d_: ph.enter_context(nc.sbuf_tensor(fresh("cv"), shp, d_))
            sets = [dict(xpad=psb([128, S + 4], BF16), cacc=psb([128, S], BF16),
                         csc=psb([128, S], BF16)) for _ in range(2)]
            setdep = [proj_done, proj_done]
            sq2 = psb([128, S], BF16)
            grow = psb([128, S], F32)
            gtmp = psb([128, S], F32)
            lrow = psb([1, S], F32)
            wdep = proj_done
            gdep = proj_done
            idx = 0
            for bh in range(H):
                rows = slice(128 * bh, 128 * (bh + 1))
                for src_d, out_d, cw in ((qT_d, qcT_d, cq_s), (kT_d, kcT_d, ck_s),
                                         (vT_d, vcT_d, cv_s), (wT_d, wcT_d, cv_s)):
                    s_i = idx % 2
                    idx += 1
                    xpad, cacc, csc = (sets[s_i]["xpad"], sets[s_i]["cacc"],
                                       sets[s_i]["csc"])
                    ld1 = p.dma("sp", xpad[:, 3:S + 3], src_d.ap()[rows, :],
                                deps=[setdep[s_i]])
                    ld2 = p.dma("gq", xpad[:, 0:3], IN["zeros128"][:, 0:3],
                                deps=[setdep[s_i]])
                    t = p.dve(lambda v, cw=cw, bh=bh, cacc=cacc, xpad=xpad: v.tensor_scalar(
                        cacc[:, :], xpad[:, 0:S], cw[:, 4 * bh:4 * bh + 1], None, ALU.mult),
                        deps=[ld1, ld2])
                    for i in (1, 2, 3):
                        t = p.dve(lambda v, cw=cw, bh=bh, i=i, cacc=cacc, xpad=xpad:
                                  v.scalar_tensor_tensor(
                                      cacc[:, :], xpad[:, i:S + i],
                                      cw[:, 4 * bh + i:4 * bh + i + 1],
                                      cacc[:, :], ALU.mult, ALU.add))
                    t = p.act(lambda s_, csc=csc, cacc=cacc: s_.activation(
                        csc[:, :], cacc[:, :], AF.Silu), deps=[t])
                    if out_d is not wcT_d:
                        sto = p.dma("sp", out_d.ap()[rows, :], csc[:, :], deps=[t])
                        setdep[s_i] = sto
                    else:
                        tq2 = p.act(lambda s_, csc=csc: s_.activation(
                            sq2[:, :], csc[:, :], AF.Square), deps=[t, wdep])
                        rowt = wdep
                        for n2 in range(8):
                            slot = n2 % 2
                            bank = P[slot]
                            mt = p.pe(lambda t_, n2=n2, bank=bank: t_.matmul(
                                bank[0:1, :], ones_cb[:, 0:1], sq2[:, 512 * n2:512 * (n2 + 1)],
                                start=True, stop=True), deps=[tq2, rowt])
                            rowt = p.dve(lambda v, n2=n2, bank=bank: v.tensor_copy(
                                lrow[0:1, 512 * n2:512 * (n2 + 1)], bank[0:1, :]), deps=[mt])
                        t = p.dve(lambda v: v.tensor_scalar(lrow[0:1, :], lrow[0:1, :],
                                                            1e-6, None, ALU.add), deps=[rowt])
                        t = p.act(lambda s_: s_.activation(lrow[0:1, :], lrow[0:1, :],
                                                           AF.Ln), deps=[t])
                        t = p.act(lambda s_: s_.activation(lrow[0:1, :], lrow[0:1, :],
                                                           AF.Exp, scale=-0.5))
                        mul_t = t
                        for n2 in range(8):
                            slot = n2 % 2
                            bank = P[slot]
                            mt = p.pe(lambda t_, n2=n2, bank=bank: t_.matmul(
                                bank[:, :], ones_r[0:1, 0:128], lrow[0:1, 512 * n2:512 * (n2 + 1)],
                                start=True, stop=True), deps=[t, mul_t])
                            mul_t = p.dve(lambda v, n2=n2, bank=bank, csc=csc: v.tensor_tensor(
                                csc[:, 512 * n2:512 * (n2 + 1)], csc[:, 512 * n2:512 * (n2 + 1)],
                                bank[:, :], ALU.mult), deps=[mt])
                        wdep = mul_t
                        sto = p.dma("sp", wcT_d.ap()[rows, :], csc[:, :], deps=[mul_t])
                        setdep[s_i] = sto
                s_i = idx % 2
                idx += 1
                cacc = sets[s_i]["cacc"]
                ld = p.dma("sp", cacc[:, :], zT_d.ap()[rows, :], deps=[setdep[s_i]])
                t = p.act(lambda s_, cacc=cacc: s_.activation(grow[:, :], cacc[:, :], AF.Abs),
                          deps=[ld, gdep])
                t = p.act(lambda s_: s_.activation(grow[:, :], grow[:, :], AF.Exp, scale=-1.0))
                t = p.dve(lambda v: v.tensor_scalar(grow[:, :], grow[:, :], 1.0, None, ALU.add),
                          deps=[t])
                t = p.act(lambda s_: s_.activation(grow[:, :], grow[:, :], AF.Ln), deps=[t])
                t = p.dve(lambda v, cacc=cacc: v.tensor_scalar(
                    gtmp[:, :], cacc[:, :], 0.0, None, ALU.min), deps=[t])
                t = p.dve(lambda v: v.tensor_tensor(gtmp[:, :], gtmp[:, :], grow[:, :],
                                                    ALU.subtract))
                gdep = t
                sto = p.dma("sp", g_d.ap()[rows, :], gtmp[:, :], deps=[t])
                setdep[s_i] = sto
            join = p.op("sp", lambda e: e.nop(), deps=[setdep[0], setdep[1]], inc=1)
            return join

    scdbg_holder = [None]

    def scan_phase(conv_done, oT):
        with ExitStack() as ph:
            psb = lambda shp, d_: ph.enter_context(nc.sbuf_tensor(fresh("sc"), shp, d_))

            def mkbufs():
                b = {}
                b["qTh"] = psb([128, S], BF16)
                b["kTh"] = psb([128, S], BF16)
                b["wTh"] = psb([128, S], BF16)
                b["brow"] = psb([1, S], BF16)
                b["Sk"] = [psb([128, 128], F32) for _ in range(2)]
                b["Sv"] = [psb([128, 128], F32) for _ in range(2)]
                for nm in ("Skb Svb einv wbt uT uS abar pa a2 a2t a1t pbar "
                           "Kc Vc gatec og").split():
                    b[nm] = psb([128, 128], BF16)
                for nm in ("gmT gsm ltm lmT dtm e2s e1s e2m t1s t2s sps o1 o2 "
                           "lend lrel tsk tsk2 tsv sqh").split():
                    b[nm] = psb([128, 128], F32)
                for nm in "mrow erow trow".split():
                    b[nm] = psb([1, 128], F32)
                for nm in "bv1 nmx zr rz ratc ss1 rr1 ecol rcol tcol".split():
                    b[nm] = psb([128, 1], F32)
                b["st"] = None
                return b

            bodies = [mkbufs() for _ in range(2)]
            bank_free = {i: None for i in range(7)}
            ptfree = {"u": None, "a2": None, "og": None}
            glob = {"otok": None}

            def head_load(b, bh):
                rows = slice(128 * bh, 128 * (bh + 1))
                prev = [conv_done, b["st"], glob["otok"]]
                b["lq"] = p.dma("sp", b["qTh"][:, :], qcT_d.ap()[rows, :], deps=prev)
                b["lk"] = p.dma("sp", b["kTh"][:, :], kcT_d.ap()[rows, :])
                b["lw"] = p.dma("sp", b["wTh"][:, :], wcT_d.ap()[rows, :])
                b["lb"] = p.dma("sp", b["brow"][0:1, :], beta_d.ap()[bh:bh + 1, :])
                b["tq"] = p.dve(lambda v, b=b: v.tensor_scalar(
                    b["qTh"][:, :], b["qTh"][:, :], SCALE_A, None, ALU.mult),
                    deps=[b["lq"]])
                z0 = p.dma("sp", b["Sk"][0][:, :], IN["zeros128"], deps=prev)
                z0 = p.dma("sp", b["Sv"][0][:, :], IN["zeros128"])
                z0 = p.dma("gq", b["Skb"][:, :], IN["zeros128"], deps=prev)
                z0 = p.dma("gq", b["Svb"][:, :], IN["zeros128"])
                b["st"] = z0

            def body(b, bh, ch):
                rows = slice(128 * bh, 128 * (bh + 1))
                cs_ = slice(128 * ch, 128 * (ch + 1))
                cur, nxt = ch % 2, 1 - ch % 2
                st_upd = b["st"]
                qTh, kTh, wTh, brow = b["qTh"], b["kTh"], b["wTh"], b["brow"]
                Sk, Sv, Skb, Svb = b["Sk"], b["Sv"], b["Skb"], b["Svb"]
                gmT, gsm, ltm, lmT = b["gmT"], b["gsm"], b["ltm"], b["lmT"]
                mrow, erow, trow, dtm = b["mrow"], b["erow"], b["trow"], b["dtm"]
                e2s, e1s, e2m, bv1 = b["e2s"], b["e1s"], b["e2m"], b["bv1"]
                einv, wbt, uT, uS = b["einv"], b["wbt"], b["uT"], b["uS"]
                abar, t1s, t2s, sps = b["abar"], b["t1s"], b["t2s"], b["sps"]
                nmx, zr, rz, pa = b["nmx"], b["zr"], b["rz"], b["pa"]
                a2, a2t, a1t, ratc = b["a2"], b["a2t"], b["a1t"], b["ratc"]
                pbar, Kc, Vc, gatec = b["pbar"], b["Kc"], b["Vc"], b["gatec"]
                ss1, rr1, o1, o2, og = b["ss1"], b["rr1"], b["o1"], b["o2"], b["og"]
                lend, lrel, ecol, rcol = b["lend"], b["lrel"], b["ecol"], b["rcol"]
                tcol, tsk, tsk2, tsv, sqh = (b["tcol"], b["tsk"], b["tsk2"],
                                             b["tsv"], b["sqh"])
                lg = p.dma("sp", gmT[:, :], g_d.ap()[rows, cs_], deps=[st_upd])
                lkc = p.tload(Kc[:, :], kcT_d.ap()[rows, cs_], deps=[st_upd])
                lvc = p.tload(Vc[:, :], vcT_d.ap()[rows, cs_], deps=[st_upd])
                lgt = p.dma("sp", gatec[:, :],
                            gate_d.ap()[cs_, 128 * bh:128 * (bh + 1)],
                            deps=[st_upd, glob["otok"]])
                t = p.pe(lambda t_: t_.transpose(P[0][:, 0:128], gmT[:, :], idf[:, :]),
                         deps=[lg, bank_free[0]])
                tg = p.dve(lambda v: v.tensor_copy(gsm[:, :], P[0][:, 0:128]), deps=[t])
                t = p.pe(lambda t_: t_.matmul(P[1][:, 0:128], ltri[:, :], gsm[:, :],
                                              start=True, stop=True),
                         deps=[tg, bank_free[1]])
                tl1 = p.dve(lambda v: v.tensor_copy(ltm[:, :], P[1][:, 0:128]), deps=[t])
                t = p.pe(lambda t_: t_.matmul(P[2][:, 0:128], gsm[:, :], ltri[:, :],
                                              start=True, stop=True),
                         deps=[tg, bank_free[2]])
                tl2 = p.dve(lambda v: v.tensor_copy(lmT[:, :], P[2][:, 0:128]), deps=[t])
                t = p.pe(lambda t_: t_.matmul(P[3][0:1, 0:128], hones[:, 0:1], gsm[:, :],
                                              start=True, stop=True),
                         deps=[tg, bank_free[3]])
                t = p.pe(lambda t_: t_.matmul(P[3][0:1, 128:256], ones_cf[:, 0:1], gsm[:, :],
                                              start=True, stop=True))
                tr = p.dve(lambda v: v.tensor_copy(mrow[0:1, :], P[3][0:1, 0:128]), deps=[t])
                tr = p.dve(lambda v: v.tensor_copy(erow[0:1, :], P[3][0:1, 128:256]))
                t = p.pe(lambda t_: t_.matmul(P[4][:, 0:128], ones_r[0:1, 0:128], mrow[0:1, :],
                                              start=True, stop=True),
                         deps=[tr, bank_free[4]])
                t = p.dve(lambda v: v.tensor_tensor(dtm[:, :], ltm[:, :], P[4][:, 0:128],
                                                    ALU.subtract), deps=[t, tl1])
                bank_free[4] = t
                te2 = p.act(lambda s_: s_.activation(e2s[:, :], dtm[:, :], AF.Exp,
                                                     bias=cst[:, 4:5]), deps=[t])
                te1 = p.act(lambda s_: s_.activation(e1s[:, :], ltm[:, :], AF.Exp), deps=[tl1])
                te2m = p.dve(lambda v: v.tensor_scalar(e2m[:, :], e2s[:, :], EM44, None,
                                                       ALU.mult), deps=[te2])
                t = p.dve(lambda v: v.tensor_scalar(bv1[:, 0:1], lmT[:, 64:65], -SHIFT,
                                                    None, ALU.add), deps=[tl2])
                tei = p.act(lambda s_: s_.activation(einv[:, :], lmT[:, :], AF.Exp,
                                                     scale=-1.0, bias=bv1[:, 0:1]), deps=[t])
                t = p.pe(lambda t_, cs_=cs_: t_.matmul(P[6][:, 256:384], ones_rb[0:1, :],
                                                       brow[0:1, cs_], start=True, stop=True),
                         deps=[b["lb"], bank_free[6]])
                t = p.dve(lambda v, cs_=cs_: v.tensor_tensor(wbt[:, :], wTh[:, cs_],
                                                             P[6][:, 256:384], ALU.mult),
                          deps=[t, b["lw"]])
                bank_free[6] = t
                tut = p.dve(lambda v: v.tensor_tensor(uT[:, :], wbt[:, :], einv[:, :],
                                                      ALU.mult), deps=[tei])
                t = p.pe(lambda t_: t_.transpose(PT[:, 128:256], uT[:, :], idb[:, :]),
                         deps=[tut, t, ptfree["u"]])
                tus = p.dve(lambda v: v.tensor_copy(uS[:, :], PT[:, 128:256]), deps=[t])
                ptfree["u"] = tus
                t = p.pe(lambda t_, cs_=cs_: t_.matmul(P[0][:, 0:128], kTh[:, cs_],
                                                       qTh[:, cs_], start=True, stop=True),
                         deps=[b["lk"], b["tq"], tg])
                tab = p.dve(lambda v: v.tensor_tensor(abar[:, :], P[0][:, 0:128], mst[:, :],
                                                      ALU.mult), deps=[t])
                t = p.pe(lambda t_: t_.matmul(P[1][:, 0:128], abar[:, :], uS[:, :],
                                              start=True, stop=True), deps=[tab, tus, tl1])
                bank_free[1] = t
                tm4 = p.pe(lambda t_, cs_=cs_: t_.matmul(P[0][:, 0:128], qTh[:, cs_],
                                                         Skb[:, :], start=True, stop=True),
                           deps=[tab, st_upd])
                bank_free[0] = tm4
                t = p.dve(lambda v: v.tensor_tensor(t1s[:, :], e1s[:, :], P[0][:, 0:128],
                                                    ALU.mult), deps=[tm4, te1])
                bank_free[0] = t
                t2_ = p.dve(lambda v: v.tensor_tensor(t2s[:, :], e2s[:, :], P[1][:, 0:128],
                                                      ALU.mult), deps=[bank_free[1], te2])
                bank_free[1] = t2_
                t = p.dve(lambda v: v.tensor_tensor(sps[:, :], t1s[:, :], t2s[:, :], ALU.add))
                t = p.dve(lambda v: v.reduce_max(nmx[:, 0:1], sps[:, :], AX.X, negate=True))
                tpa = p.act(lambda s_: s_.activation(pa[:, :], sps[:, :], AF.Exp,
                                                     bias=nmx[:, 0:1], accum_out=zr[:, 0:1]),
                            deps=[t])
                t = p.dve(lambda v: v.reciprocal(rz[:, 0:1], zr[:, 0:1]), deps=[tpa])
                ta2 = p.dve(lambda v: v.scalar_tensor_tensor(a2[:, :], pa[:, :], rz[:, 0:1],
                                                             e2m[:, :], ALU.mult, ALU.mult),
                            deps=[te2m])
                t = p.pe(lambda t_: t_.transpose(PT[:, 256:384], a2[:, :], idb[:, :]),
                         deps=[ta2, tus, ptfree["a2"]])
                ta2t = p.dve(lambda v: v.tensor_copy(a2t[:, :], PT[:, 256:384]), deps=[t])
                ptfree["a2"] = ta2t
                trat = p.act(lambda s_: s_.activation(ratc[:, 0:1], lmT[:, 64:65], AF.Exp,
                                                      bias=cst[:, 4:5]), deps=[tl2])
                ta1t = p.dve(lambda v: v.tensor_scalar(a1t[:, :], a2t[:, :], ratc[:, 0:1],
                                                       None, ALU.mult), deps=[ta2t, trat])
                t = p.pe(lambda t_: t_.matmul(P[2][:, 0:128], uT[:, :], a2t[:, :],
                                              start=True, stop=True),
                         deps=[ta2t, tut, tl2, bank_free[2]])
                tpb = p.dve(lambda v: v.scalar_tensor_tensor(pbar[:, :], mst[:, :],
                                                             E44, P[2][:, 0:128], ALU.mult,
                                                             ALU.mult), deps=[t])
                bank_free[2] = tpb
                t = p.pe(lambda t_: t_.matmul(P[5][:, 0:128], pbar[:, :], Vc[:, :],
                                              start=True, stop=False),
                         deps=[tpb, lvc, bank_free[5]])
                to = p.pe(lambda t_: t_.matmul(P[5][:, 0:128], a1t[:, :], Svb[:, :],
                                               start=False, stop=True),
                          deps=[ta1t, st_upd])
                t = p.act(lambda s_: s_.activation(sqh[:, :], P[5][:, 0:128], AF.Square,
                                                   accum_out=ss1[:, 0:1]), deps=[to])
                t = p.dve(lambda v: v.tensor_scalar(ss1[:, 0:1], ss1[:, 0:1], 1.0 / 128.0,
                                                    None, ALU.mult), deps=[t])
                t = p.act(lambda s_: s_.activation(rr1[:, 0:1], ss1[:, 0:1], AF.Ln,
                                                   bias=cst[:, 0:1]), deps=[t])
                t = p.act(lambda s_: s_.activation(rr1[:, 0:1], rr1[:, 0:1], AF.Exp,
                                                   scale=-0.5))
                t = p.dve(lambda v: v.tensor_scalar(o1[:, :], P[5][:, 0:128], rr1[:, 0:1],
                                                    None, ALU.mult), deps=[t])
                bank_free[5] = t
                t = p.dve(lambda v: v.tensor_tensor(o2[:, :], o1[:, :], nwb[:, :], ALU.mult))
                t = p.dve(lambda v: v.tensor_tensor(og[:, :], o2[:, :], gatec[:, :],
                                                    ALU.mult), deps=[lgt])
                t = p.pe(lambda t_: t_.transpose(PT[:, 384:512], og[:, :], idb[:, :]),
                         deps=[t, ta2t, ptfree["og"]])
                otok = p.dve(lambda v, bh=bh, cs_=cs_: v.tensor_copy(
                    oT[bh][:, cs_], PT[:, 384:512]), deps=[t, glob["otok"]])
                glob["otok"] = otok
                ptfree["og"] = otok
                t = p.pe(lambda t_: t_.matmul(P[3][:, 0:128], Kc[:, :], uS[:, :],
                                              start=True, stop=True),
                         deps=[lkc, tus, tr, bank_free[3]])
                tdk = t
                t = p.pe(lambda t_: t_.matmul(P[4][:, 0:128], uS[:, :], Vc[:, :],
                                              start=True, stop=True),
                         deps=[tus, lvc, bank_free[4]])
                tdv = t
                t = p.dve(lambda v: v.tensor_tensor(trow[0:1, :], erow[0:1, :], mrow[0:1, :],
                                                    ALU.subtract), deps=[tr])
                t = p.dve(lambda v: v.tensor_scalar(trow[0:1, :], trow[0:1, :],
                                                    SHIFT, None, ALU.add), deps=[t])
                tlr = p.act(lambda s_: s_.activation(trow[0:1, :], trow[0:1, :], AF.Exp),
                            deps=[t])
                tle = p.act(lambda s_: s_.activation(erow[0:1, :], erow[0:1, :], AF.Exp),
                            deps=[t])
                t = p.pe(lambda t_: t_.matmul(P[6][:, 0:128], ones_r[0:1, 0:128], erow[0:1, :],
                                              start=True, stop=True),
                         deps=[tle, bank_free[6]])
                t = p.pe(lambda t_: t_.matmul(P[6][:, 128:256], ones_r[0:1, 0:128], trow[0:1, :],
                                              start=True, stop=True), deps=[tlr])
                t = p.dve(lambda v: v.tensor_copy(lend[:, :], P[6][:, 0:128]), deps=[t])
                t = p.dve(lambda v: v.tensor_copy(lrel[:, :], P[6][:, 128:256]))
                bank_free[6] = t
                tcolv = p.act(lambda s_: s_.activation(ecol[:, 0:1], lmT[:, 127:128],
                                                       AF.Exp), deps=[tl2])
                t = p.dve(lambda v: v.tensor_tensor(tcol[:, 0:1], lmT[:, 127:128],
                                                    lmT[:, 64:65], ALU.subtract), deps=[tl2])
                trc = p.act(lambda s_: s_.activation(rcol[:, 0:1], tcol[:, 0:1], AF.Exp,
                                                     bias=cst[:, 4:5]), deps=[t])
                t = p.dve(lambda v: v.tensor_tensor(tsk[:, :], P[3][:, 0:128], lrel[:, :],
                                                    ALU.mult), deps=[tdk, bank_free[6]])
                bank_free[3] = t
                t = p.dve(lambda v, cur=cur: v.tensor_tensor(tsk2[:, :], Sk[cur][:, :],
                                                             lend[:, :], ALU.mult))
                t = p.dve(lambda v, nxt=nxt: v.tensor_tensor(Sk[nxt][:, :], tsk[:, :],
                                                             tsk2[:, :], ALU.add))
                tskb = p.dve(lambda v, nxt=nxt: v.tensor_copy(Skb[:, :], Sk[nxt][:, :]),
                             deps=[tm4, t])
                t = p.dve(lambda v: v.tensor_scalar(tsv[:, :], P[4][:, 0:128], rcol[:, 0:1],
                                                    None, ALU.mult), deps=[tdv, trc])
                bank_free[4] = t
                t = p.dve(lambda v, cur=cur, nxt=nxt: v.scalar_tensor_tensor(
                    Sv[nxt][:, :], Sv[cur][:, :], ecol[:, 0:1], tsv[:, :],
                    ALU.mult, ALU.add), deps=[tcolv, to])
                tsvb = p.dve(lambda v, nxt=nxt: v.tensor_copy(Svb[:, :], Sv[nxt][:, :]),
                             deps=[t])
                b["st"] = tsvb
                bank_free[0] = tskb

            for bh in range(H):
                head_load(bodies[bh % 2], bh)
                for ch in range(32):
                    body(bodies[bh % 2], bh, ch)
            return glob["otok"]
    def ln2_gate(h3buf, ag_free, entry=None):
        with ExitStack() as ph:
            psb = lambda shp, d_: ph.enter_context(nc.sbuf_tensor(fresh("l2"), shp, d_))
            lf = [psb([128, D], F32) for _ in range(2)]
            lnb = psb([128, D], BF16)
            ln2w_f = psb([128, D], F32)
            ln2b_f = psb([128, D], F32)
            combT_sb = psb([4, TOK], F32)
            lw1 = p.dma("sp", ln2w_f[:, :], IN["ln2w"], deps=[entry])
            lw2 = p.dma("sp", ln2b_f[:, :], IN["ln2b"])
            ltb = [psb([128, 128], F32) for _ in range(2)]
            gt4 = psb([128, 4], F32)
            er = psb([128, 4], F32)
            rr4 = psb([128, 4], F32)
            sel4 = psb([128, 4], F32)
            cb4 = psb([128, 4], F32)
            c1 = psb([128, 1], F32)
            c2 = psb([128, 1], F32)
            c3 = psb([128, 1], F32)
            vt, at = [], []
            lwx = p.dma("sp", ln2w_f[:, :], IN["ln2w"], deps=[entry]) if False else None
            for i in range(NT):
                dep = [p.now("dve"), entry] if i == 0 else []
                vt.append(p.dve(lambda v, i=i: v.reduce_sum(st1[:, i:i + 1], h3buf[i][:, :], AX.X),
                                deps=dep))
                at.append(p.act(lambda s, i=i: s.activation(
                    sqd[:, :], h3buf[i][:, :], AF.Square, accum_out=st2[:, i:i + 1]), deps=dep))
            ok = []
            for i in range(NT):
                p.dve(lambda v, i=i: v.tensor_scalar(st1[:, i:i + 1], st1[:, i:i + 1],
                                                     1.0 / D, None, ALU.mult), deps=[vt[i]])
                p.dve(lambda v, i=i: v.tensor_scalar(st2[:, i:i + 1], st2[:, i:i + 1],
                                                     1.0 / D, None, ALU.mult), deps=[at[i]])
                p.dve(lambda v, i=i: v.scalar_tensor_tensor(
                    st2[:, i:i + 1], st1[:, i:i + 1], st1[:, i:i + 1], st2[:, i:i + 1],
                    ALU.mult, ALU.subtract))
                t = p.dve(lambda v, i=i: v.tensor_scalar(st2[:, i:i + 1], st2[:, i:i + 1],
                                                         -1.0, None, ALU.mult))
                t = p.act(lambda s, i=i: s.activation(st3[:, i:i + 1], st2[:, i:i + 1],
                                                      AF.Ln, bias=cst[:, 0:1]), deps=[t])
                t = p.act(lambda s, i=i: s.activation(rr_t[:, i:i + 1], st3[:, i:i + 1],
                                                      AF.Exp, scale=-0.5))
                t = p.dve(lambda v, i=i: v.tensor_scalar(
                    nmr_t[:, i:i + 1], st1[:, i:i + 1], rr_t[:, i:i + 1], -1.0,
                    ALU.mult, ALU.mult), deps=[t])
                ok.append(t)
            sto = ag_free
            combtok = None
            bank_free = {0: None, 1: None, 2: None, 3: None, 4: None}
            for i in range(NT):
                sl = i % 2
                t = p.dve(lambda v, i=i, sl=sl: v.tensor_scalar(
                    lf[sl][:, :], h3buf[i][:, :], rr_t[:, i:i + 1], nmr_t[:, i:i + 1],
                    ALU.mult, ALU.add), deps=[ok[i], bank_free.get("lf%d" % sl)])
                t = p.dve(lambda v, sl=sl: v.tensor_tensor(lf[sl][:, :], lf[sl][:, :],
                                                           ln2w_f[:, :], ALU.mult), deps=[lw2])
                tf = p.dve(lambda v, sl=sl: v.tensor_tensor(lf[sl][:, :], lf[sl][:, :],
                                                            ln2b_f[:, :], ALU.add))
                t = p.dve(lambda v, sl=sl: v.tensor_copy(lnb[:, :], lf[sl][:, :]), deps=[sto])
                sto = p.dma("sp", ag_in.ap()[128 * i:128 * (i + 1), :], lnb[:, :], deps=[t])
                gp = P[2 + (i % 2)]
                gslot = 2 + (i % 2)
                mmt = None
                for k in range(16):
                    ts_ = k % 2
                    tt_ = p.pe(lambda t_, sl=sl, k=k, ts_=ts_: t_.transpose(
                        P[ts_][:, 0:128], lf[sl][:, 128 * k:128 * (k + 1)], idf[:, :]),
                        deps=[tf, bank_free[ts_]])
                    tc = p.dve(lambda v, ts_=ts_: v.tensor_copy(ltb[ts_][:, :], P[ts_][:, 0:128]),
                               deps=[tt_])
                    bank_free[ts_] = tc
                    mmt = p.pe(lambda t_, k=k, ts_=ts_, gp=gp: t_.matmul(
                        gp[:, 0:4], ltb[ts_][:, :], wgate_s[:, 4 * k:4 * (k + 1)],
                        start=(k == 0), stop=(k == 15)), deps=[tc, bank_free[gslot]])
                tg4 = p.dve(lambda v, gp=gp: v.tensor_copy(gt4[:, :], gp[:, 0:4]), deps=[mmt])
                bank_free[gslot] = tg4
                t = p.dve(lambda v: v.reduce_max(c1[:, 0:1], gt4[:, :], AX.X, negate=True))
                t = p.act(lambda s_: s_.activation(er[:, :], gt4[:, :], AF.Exp,
                                                   bias=c1[:, 0:1], accum_out=c2[:, 0:1]),
                          deps=[t])
                t = p.dve(lambda v: v.reciprocal(c3[:, 0:1], c2[:, 0:1]), deps=[t])
                t = p.dve(lambda v: v.tensor_scalar(rr4[:, :], er[:, :], c3[:, 0:1], None,
                                                    ALU.mult))
                for e in range(4):
                    js = [j for j in range(4) if j != e]
                    t = p.dve(lambda v, e=e, j=js[0]: v.tensor_tensor(
                        sel4[:, e:e + 1], rr4[:, j:j + 1], rr4[:, e:e + 1], ALU.is_gt))
                    for j in js[1:]:
                        t = p.dve(lambda v, e=e, j=j: v.tensor_tensor(
                            c1[:, 0:1], rr4[:, j:j + 1], rr4[:, e:e + 1], ALU.is_gt))
                        t = p.dve(lambda v, e=e: v.tensor_tensor(
                            sel4[:, e:e + 1], sel4[:, e:e + 1], c1[:, 0:1], ALU.add))
                t = p.dve(lambda v: v.tensor_scalar(sel4[:, :], sel4[:, :], 1.5, None, ALU.is_lt))
                t = p.dve(lambda v: v.reduce_max(c1[:, 0:1], rr4[:, :], AX.X, negate=True),
                          deps=[t])
                t = p.act(lambda s_: s_.activation(er[:, :], rr4[:, :], AF.Exp,
                                                   bias=c1[:, 0:1]), deps=[t])
                t = p.dve(lambda v: v.tensor_tensor(er[:, :], er[:, :], sel4[:, :], ALU.mult),
                          deps=[t])
                t = p.dve(lambda v: v.reduce_sum(c2[:, 0:1], er[:, :], AX.X))
                t = p.dve(lambda v: v.reciprocal(c3[:, 0:1], c2[:, 0:1]))
                t = p.dve(lambda v: v.tensor_scalar(cb4[:, :], er[:, :], c3[:, 0:1], None,
                                                    ALU.mult))
                tt_ = p.pe(lambda t_: t_.transpose(P[4][0:4, 0:128], cb4[:, :], idf[:, :]),
                           deps=[t, bank_free[4]])
                combtok = p.dve(lambda v, i=i: v.tensor_copy(
                    combT_sb[0:4, 128 * i:128 * (i + 1)], P[4][0:4, 0:128]), deps=[tt_, combtok])
                bank_free[4] = combtok
                bank_free["lf%d" % sl] = tg4
            csto = p.dma("sp", combT_d.ap()[0:4, :], combT_sb[0:4, :], deps=[combtok])
            return sto, csto

    def moe(cc_ln2, cc_comb, rs_free, rsout_free, join):
        with ExitStack() as ph:
            psb = lambda shp, d_: ph.enter_context(nc.sbuf_tensor(fresh("mo"), shp, d_))
            cg = psb([16, TOK], F32)
            comb_row = psb([1, S], F32)
            lcg = p.dma("sp", cg[:, :], combT_g.ap(), deps=[cc_comb, join])
            crt = None
            for r in range(4):
                for n2 in range(2):
                    slot = (2 * r + n2) % 2
                    mt = p.pe(lambda t_, r=r, n2=n2, slot=slot: t_.matmul(
                        P[slot][0:1, :], onehot_s[:, r:r + 1],
                        cg[:, 512 * n2:512 * (n2 + 1)], start=True, stop=True),
                        deps=[lcg, crt])
                    crt = p.dve(lambda v, r=r, n2=n2, slot=slot: v.tensor_copy(
                        comb_row[0:1, TOK * r + 512 * n2:TOK * r + 512 * (n2 + 1)],
                        P[slot][0:1, :]), deps=[mt])
            with ExitStack() as ph1:
                psb1 = lambda shp, d_: ph1.enter_context(nc.sbuf_tensor(fresh("m1"), shp, d_))
                we1_sb = psb1([128, 16 * D], BF16)
                rhsT = psb1([128, 8192], BF16)
                cbc = psb1([128, 512], BF16)
                hm1 = [psb1([128, 512], BF16) for _ in range(2)]
                hm2 = [psb1([128, 512], BF16) for _ in range(2)]
                wt = cast_w(we1_sb, IN["we1e"], D, nk=16, nsplit=8, deps=[join, cc_ln2])
                prev = [None, None]
                sto = join
                cbtok = None
                for r in range(4):
                    for hh in range(2):
                        rt = load_rhsT(rhsT, r, hh, cc_ln2, extra=[sto])
                        c0 = TOK * r + 512 * hh
                        mt = p.pe(lambda t_, c0=c0: t_.matmul(
                            P[2][:, 0:512], ones_r[0:1, 0:128], comb_row[0:1, c0:c0 + 512],
                            start=True, stop=True), deps=[crt, cbtok])
                        tc = p.dve(lambda v: v.tensor_copy(cbc[:, :], P[2][:, 0:512]), deps=[mt])
                        cbtok = tc
                        for It in range(16):
                            slot = It % 2
                            bank = P[slot]

                            def mmf(t, It=It, bank=bank):
                                ins = None
                                for k in range(16):
                                    ins = t.matmul(bank[:, :],
                                                   we1_sb[:, D * k + 128 * It:D * k + 128 * (It + 1)],
                                                   rhsT[:, 512 * k:512 * (k + 1)],
                                                   start=(k == 0), stop=(k == 15))
                                return ins
                            mt2 = p.pe(mmf, deps=[rt, wt, prev[slot]])
                            t = p.act(lambda s_, It=It, bank=bank, slot=slot: s_.activation(
                                hm1[slot][:, :], bank[:, :], AF.Gelu, bias=be1_s[:, It:It + 1]),
                                deps=[mt2])
                            prev[slot] = t
                            t = p.dve(lambda v, slot=slot: v.tensor_tensor(
                                hm2[slot][:, :], hm1[slot][:, :], cbc[:, :], ALU.mult),
                                deps=[t, tc, sto])
                            sto = p.dma("sp",
                                        hm_d.ap()[128 * It:128 * (It + 1), c0:c0 + 512],
                                        hm2[slot][:, :], deps=[t])
            with ExitStack() as ph2:
                psb2 = lambda shp, d_: ph2.enter_context(nc.sbuf_tensor(fresh("m2"), shp, d_))
                we2_sb = psb2([128, 16 * D], BF16)
                ht = psb2([128, 16 * 128], BF16)
                be2r = psb2([1, D], F32)
                lb2 = p.dma("sp", be2r[:, :], IN["be2e"])
                stg = [psb2([128, 512], F32) for _ in range(2)]
                wt2 = cast_w(we2_sb, IN["we2e"], D, nk=16, nsplit=8, deps=[sto])
                prev = [None, None]
                sto2 = rs_free
                lh = sto
                for tt in range(32):
                    lh = p.dma("sp", ht[:, :].rearrange("p (k j) -> p k j", j=128),
                               hm_d.ap()[:, 128 * tt:128 * (tt + 1)].rearrange(
                                   "(k p) j -> p k j", p=128),
                               deps=[sto, prev[0], prev[1]])
                    for n in range(4):
                        slot = n % 2
                        bank = P[slot]

                        def mmf2(t, tt=tt, n=n, bank=bank):
                            for k in range(16):
                                t.matmul(bank[:, :], ht[:, 128 * k:128 * (k + 1)],
                                         we2_sb[:, D * k + 512 * n:D * k + 512 * (n + 1)],
                                         start=(k == 0), stop=False)
                            return t.matmul(bank[:, :],
                                            comb_row[0:1, 128 * tt:128 * (tt + 1)],
                                            be2r[0:1, 512 * n:512 * (n + 1)],
                                            start=False, stop=True)
                        mt = p.pe(mmf2, deps=[lh, wt2, lb2, crt, prev[slot]])
                        sg = stg[slot]
                        t = p.dve(lambda v, bank=bank, sg=sg: v.tensor_copy(sg[:, :], bank[:, :]),
                                  deps=[mt, sto2])
                        prev[slot] = t
                        sto2 = p.dma("sp",
                                     rs_in.ap()[128 * tt:128 * (tt + 1), 512 * n:512 * (n + 1)],
                                     sg[:, :], deps=[t])
                cc = p.coll("ReduceScatter", ALU.add, [rs_in.ap().opt()], [rs_out.ap().opt()],
                            deps=[sto2, rsout_free])
            return cc

    state = {"x_src": x_in, "ag_free": SET, "agout_free": SET, "entry": SET,
             "rs_ccs": [SET] * 4, "rsout_free_c": [SET] * 4,
             "rs_free_moe": SET, "rsout_free_moe": SET}

    def sub_attn_a(final):
        lnsto, _ = ln_phase(state["x_src"], "plain", state["ag_free"], entry=state["entry"])
        cc = p.coll("AllGather", ALU.bypass, [ag_in.ap().opt()], [ag_out.ap().opt()],
                    deps=[lnsto, state["agout_free"]])
        join = p.now("sp")
        projdone = proj_a(cc, join)
        convdone = conv_phase(projdone)
        with ExitStack() as ph:
            oTbig = ph.enter_context(nc.sbuf_tensor(fresh("oTa"), [128, 4 * S], BF16))
            oT = [oTbig[:, S * h:S * (h + 1)] for h in range(4)]
            otok = scan_phase(convdone, oT)
            if "oT_dbg" in _DEBUG:
                for bh in range(4):
                    otok = p.dma("sp", oT_dbg.ap()[128 * bh:128 * (bh + 1), :], oT[bh][:, :],
                                 deps=[otok])
            cc2 = out_proj_rs(oTbig, "woa_s", otok, state["rs_ccs"], state["rsout_free_c"])
            last, cfree = residual(state["x_src"], cc2, "plain", final=final)
        state["x_src"] = x_dram.ap()
        state["ag_free"] = cc
        state["agout_free"] = p.now("tq")
        state["rs_ccs"] = cc2
        state["rsout_free_c"] = cfree
        state["entry"] = last
        return last

    def sub_softmax_core(wqn, wkn, wvn, bqn, bkn, bvn, won, causal, lnmode):
        lnsto, _ = ln_phase(state["x_src"], lnmode, state["ag_free"], entry=state["entry"])
        cc = p.coll("AllGather", ALU.bypass, [ag_in.ap().opt()], [ag_out.ap().opt()],
                    deps=[lnsto, state["agout_free"]])
        seed = p.now("sp")
        projdone = proj_qkv(IN[wqn], IN[wkn], IN[wvn], bqn, bkn, bvn, cc, seed)
        with ExitStack() as ph:
            oTbig = ph.enter_context(nc.sbuf_tensor(fresh("oTs"), [128, 4 * S], BF16))
            oT = [oTbig[:, S * h:S * (h + 1)] for h in range(4)]
            otok = attn_inner(causal, oT, projdone)
            if "oT_dbg" in _DEBUG:
                for bh in range(4):
                    otok = p.dma("sp", oT_dbg.ap()[128 * bh:128 * (bh + 1), :], oT[bh][:, :],
                                 deps=[otok])
            cc2 = out_proj_rs(oTbig, won, otok, state["rs_ccs"], state["rsout_free_c"])
        state["x_src"] = x_dram.ap()
        state["ag_free"] = cc
        state["agout_free"] = p.now("tq")
        state["rs_ccs"] = cc2
        return cc2

    def sub_attn_b(final):
        xsrc = state["x_src"]
        cc2 = sub_softmax_core("wqb", "wkb", "wvb", "bqb", "bkb", "bvb",
                               "wob_s", False, "plain")
        last, cfree = residual(xsrc, cc2, "plain", final=final, bias=rbias["obb_bc"])
        state["rsout_free_c"] = cfree
        state["entry"] = last
        return last

    def sub_ffn(final):
        xsrc = state["x_src"]
        cc2 = sub_softmax_core("wqf", "wkf", "wvf", "bqf", "bkf", "bvf",
                               "wof_s", True, "ffn")
        with ExitStack() as ph:
            h3buf = [ph.enter_context(nc.sbuf_tensor(fresh("h3"), [128, D], F32))
                     for _ in range(NT)]
            rtok, cfree = residual(xsrc, cc2, "ffn", h3buf=h3buf, bias=rbias["bof_bc"])
            state["rsout_free_c"] = cfree
            ln2sto, csto = ln2_gate(h3buf, state["ag_free"], entry=rtok)
        cc3 = p.coll("AllGather", ALU.bypass, [ag_in.ap().opt()], [ag_out.ap().opt()],
                     deps=[ln2sto, state["agout_free"]])
        cc4 = p.coll("AllGather", ALU.bypass, [combT_d.ap().opt()], [combT_g.ap().opt()],
                     deps=[csto])
        cc5 = moe(cc3, cc4, state["rs_free_moe"], state["rsout_free_moe"], p.now("sp"))
        last, mfree = residual(x_dram.ap(), cc5, "plain", final=final)
        state["ag_free"] = cc3
        state["agout_free"] = p.now("tq")
        state["rs_free_moe"] = cc5
        state["rsout_free_moe"] = mfree
        state["entry"] = last
        return last

    subs = [sub_attn_a, sub_ffn, sub_attn_b, sub_ffn]
    last = None
    for si in range(_TRUNC):
        last = subs[si](si == _TRUNC - 1)

    dbg_map = {"ag_in": ag_in, "ag_out": ag_out, "qT_d": qT_d, "kT_d": kT_d,
               "vT_d": vT_d, "wT_d": wT_d, "zT_d": zT_d, "v_tm": v_tm,
               "qcT_d": qcT_d, "kcT_d": kcT_d, "vcT_d": vcT_d, "wcT_d": wcT_d,
               "g_d": g_d, "beta_d": beta_d, "gate_d": gate_d, "hm_d": hm_d,
               "rs_in": rs_in, "rs_out": rs_out, "x_dram": x_dram,
               "combT_g": combT_g, "oT_dbg": oT_dbg, "sc_dbg": sc_dbg}
    for nm in _DEBUG:
        th = dbg_map[nm]
        shp = list(th.ap().tensor.shape)
        od = dt("dbg_" + nm, shp, th.ap().tensor.dtype, kind="ExternalOutput")
        last = p.dma("sp", od.ap(), th.ap(), deps=[last])

    p.op("sp", lambda e: e.nop(), deps=[last], inc=0)

    p.emit(block)
    blockstk.close()
    stk.close()
    p.stk.close()
    return nc


def kernel(**inputs):
    per_core = _prep(inputs)
    nc = build()
    res = run_bass_kernel_spmd(nc, per_core, core_ids=list(range(NC)))
    outs = [res.results[c]["y"] for c in range(NC)]
    if _DEBUG:
        kernel.dbg = res.results
    return np.concatenate(outs, axis=0).reshape(2, 4096, D)


# version 14 (final)
# speedup vs baseline: 1.1238x; 1.1238x over previous
import numpy as np
import concourse.bass as bass
import concourse.mybir as mybir
from concourse.bass_utils import run_bass_kernel_spmd
from contextlib import ExitStack

F32 = mybir.dt.float32
BF16 = mybir.dt.bfloat16
AF = mybir.ActivationFunctionType
ALU = mybir.AluOpType
AX = mybir.AxisListType

NC = 8
TOK = 1024
NT = 8
D = 2048
S = 4096
H = 4
DH = 128
GROUPS = [[0, 1, 2, 3], [4, 5, 6, 7]]
LN_EPS = 1e-5
SHIFT = 22.0
E44 = float(np.exp(44.0))
EM44 = float(np.exp(-44.0))
SCALE_A = float(128.0 ** -0.5)
CSHIFT = 20.0

_TRUNC = 4
_DEBUG = ()

STREAMS = ("pe", "dve", "sp", "sc", "gp")
SEMS = ("pe", "dve", "sp", "act", "tq", "gq", "pool")
SEM2STREAM = {"pe": "pe", "dve": "dve", "sp": "sp", "act": "sc", "tq": "sc",
              "gq": "gp", "pool": "gp"}


def _rope_tables():
    inv = 1.0 / (10000.0 ** (np.arange(0, DH, 2, dtype=np.float64) / DH))
    t = np.arange(S, dtype=np.float64)
    fr = t[:, None] * inv[None, :]
    emb = np.concatenate([fr, fr], axis=-1)
    cos = np.ascontiguousarray(np.cos(emb).T.astype(np.float32))
    sin = np.sin(emb).T.astype(np.float32)
    sin[:64, :] *= -1.0
    return cos, np.ascontiguousarray(sin)


def _prep(I):
    x = np.asarray(I["L0_attn_a_hidden_states"], np.float32).reshape(8192, D)
    cos, sinp = _rope_tables()
    Ltri = np.ascontiguousarray(np.tril(np.ones((128, 128), np.float32)).T)
    half = np.ascontiguousarray((np.arange(128) <= 64).astype(np.float32)[:, None])
    ident = np.eye(128, dtype=np.float32)
    kl = np.arange(128)[:, None]
    ql = np.arange(512)[None, :]
    cmall = np.concatenate([(ql >= kl + dlt).astype(np.float32)
                            for dlt in (0, 128, 256, 384)], axis=1)

    A = lambda v: np.ascontiguousarray(np.asarray(v).astype(np.float32))
    consts = np.zeros((128, 16), np.float32)
    consts[:, 0] = LN_EPS
    consts[:, 1] = 1.0 + LN_EPS
    consts[:, 2] = 1e-6
    consts[:, 3] = 1.0
    consts[:, 4] = SHIFT
    consts[:, 5] = -CSHIFT
    consts[:, 6] = 1.0
    shared = dict(
        consts_in=consts,
        ones_row=np.ones((1, 512), np.float32),
        zeros_row=np.zeros((1, 512), np.float32),
        zeros128=np.zeros((128, 128), np.float32),
        vaug_ones=np.ones((128, 32 * 130), np.float32),
        cos_t=cos, sin_t=sinp, Ltri=Ltri, halfones=half, ident=ident, cmall=cmall,
        wf1=A(I["L0_attn_a_Wf1"]), wg1=A(I["L0_attn_a_Wg1"]),
        ln1w=A(np.broadcast_to(I["L0_ffn_ln1_w"], (128, D))),
        ln1b=A(np.broadcast_to(I["L0_ffn_ln1_b"], (128, D))),
        ln2w=A(np.broadcast_to(I["L0_ffn_ln2_w"], (128, D))),
        ln2b=A(np.broadcast_to(I["L0_ffn_ln2_b"], (128, D))),
        normw=A(np.broadcast_to(I["L0_attn_a_norm_w"], (128, DH))),
        wgate=A(np.asarray(I["L0_ffn_w_gate"]).reshape(16, 128, 4).transpose(1, 0, 2).reshape(128, 64)),
    )
    wob_full = np.asarray(I["L0_attn_b_out_w"]).T

    per_core = []
    for c in range(NC):
        g = c % 4
        cs = slice(512 * g, 512 * (g + 1))
        d = dict(shared)
        d["x"] = A(x[TOK * c:TOK * (c + 1)])
        d["wqa"] = A(I["L0_attn_a_Wq"][:, cs])
        d["wka"] = A(I["L0_attn_a_Wk"][:, cs])
        d["wva"] = A(I["L0_attn_a_Wv"][:, cs])
        d["wwa"] = A(I["L0_attn_a_Ww"][:, cs])
        for nm, key in (("cqa", "L0_attn_a_cq"), ("cka", "L0_attn_a_ck"),
                        ("cva", "L0_attn_a_cv")):
            d[nm] = A(np.asarray(I[key])[cs].reshape(4, 128, 4).transpose(1, 0, 2).reshape(128, 16))
        d["wf2s"] = A(I["L0_attn_a_Wf2"][:, cs])
        d["wg2s"] = A(I["L0_attn_a_Wg2"][:, cs])
        d["bg2s"] = A(np.broadcast_to(np.asarray(I["L0_attn_a_bg2"])[cs], (128, 512)))
        d["wbs"] = A(I["L0_attn_a_Wb"][:, 4 * g:4 * (g + 1)])
        d["woa_s"] = A(np.asarray(I["L0_attn_a_Wo"])[cs, :])
        d["wqb"] = A(np.asarray(I["L0_attn_b_q_w"])[cs, :].T)
        d["wkb"] = A(np.asarray(I["L0_attn_b_k_w"])[cs, :].T)
        d["wvb"] = A(np.asarray(I["L0_attn_b_v_w"])[cs, :].T)
        d["bqb"] = A(np.asarray(I["L0_attn_b_q_b"])[cs].reshape(4, 128).T)
        d["bkb"] = A(np.asarray(I["L0_attn_b_k_b"])[cs].reshape(4, 128).T)
        d["bvb"] = A(np.broadcast_to(np.asarray(I["L0_attn_b_v_b"])[cs], (128, 512)))
        d["wob_s"] = A(wob_full[cs, :])
        d["obb_bc"] = A(np.broadcast_to(np.asarray(I["L0_attn_b_out_b"]), (128, D)))
        d["wqf"] = A(I["L0_ffn_wq"][:, cs])
        d["wkf"] = A(I["L0_ffn_wk"][:, cs])
        d["wvf"] = A(I["L0_ffn_wv"][:, cs])
        d["bqf"] = A(np.asarray(I["L0_ffn_bq"])[cs].reshape(4, 128).T)
        d["bkf"] = A(np.asarray(I["L0_ffn_bk"])[cs].reshape(4, 128).T)
        d["bvf"] = A(np.broadcast_to(np.asarray(I["L0_ffn_bv"])[cs], (128, 512)))
        d["wof_s"] = A(np.asarray(I["L0_ffn_wo"])[cs, :])
        d["bof_bc"] = A(np.broadcast_to(np.asarray(I["L0_ffn_bo"]), (128, D)))
        d["we1e"] = A(np.asarray(I["L0_ffn_we1"])[g])
        d["we2e"] = A(np.asarray(I["L0_ffn_we2"])[g])
        d["be1e"] = A(np.asarray(I["L0_ffn_be1"])[g].reshape(16, 128).T)
        d["be2e"] = A(np.asarray(I["L0_ffn_be2"])[g][None, :])
        oh = np.zeros((16, 4), np.float32)
        for r in range(4):
            oh[4 * r + g, r] = 1.0
        d["onehots"] = oh
        per_core.append(d)
    return per_core


class Prog:

    def __init__(self, nc):
        self.nc = nc
        self.seq = {s: [] for s in STREAMS}
        self.cnt = {s: 0 for s in SEMS}
        self.stk = ExitStack()
        self.sem = {s: self.stk.enter_context(nc.semaphore(s + "_s")) for s in SEMS}
        self.ccs = []

    def _depmap(self, deps):
        m = {}
        for dd in deps:
            if dd is None:
                continue
            s, v = dd
            m[s] = max(m.get(s, 0), v)
        return m

    def op(self, sem_name, emit, deps=(), inc=1):
        stream = SEM2STREAM[sem_name]
        deps = list(deps)
        if sem_name in ("dve", "act", "pool") and self.cnt[sem_name] > 0:
            deps.append((sem_name, self.cnt[sem_name]))
        self.cnt[sem_name] += inc
        self.seq[stream].append((emit, self._depmap(deps), (sem_name, inc)))
        return (sem_name, self.cnt[sem_name])

    def pe(self, emit, deps=()):
        return self.op("pe", emit, deps)

    def dve(self, emit, deps=()):
        return self.op("dve", emit, deps)

    def act(self, emit, deps=()):
        return self.op("act", emit, deps)

    def pool(self, emit, deps=()):
        return self.op("pool", emit, deps)

    def dma(self, q, dst, src, deps=(), **kw):
        return self.op(q, lambda e: e.dma_start(dst, src, **kw), deps, inc=16)

    def tload(self, dst, src, deps=()):
        return self.op("tq", lambda e: e.dma_start(dst, src, transpose=True), deps, inc=16)

    def coll(self, kind, alu, ins, outs, deps=()):
        h = self.stk.enter_context(self.nc.semaphore(f"cc{len(self.ccs)}_s"))
        self.ccs.append(h)
        idx = len(self.ccs) - 1

        def emit(e):
            return e.collective_compute(kind, alu, replica_groups=GROUPS,
                                        ins=ins, outs=outs)
        self.seq["gp"].append((emit, self._depmap(deps), ("cc%d" % idx, 1)))
        return ("cc%d" % idx, 1)

    def now(self, sem_name):
        return (sem_name, self.cnt[sem_name])

    def emit(self, block):
        def runner(stream):
            def run(eng):
                seen = {}
                for emit, deps, inc in self.seq[stream]:
                    for s, v in deps.items():
                        if seen.get(s, -1) >= v:
                            continue
                        seen[s] = v
                        h = self.ccs[int(s[2:])] if s.startswith("cc") else self.sem[s]
                        eng.wait_ge(h, v)
                    ins = emit(eng)
                    if ins is not None and inc[1]:
                        s, k = inc
                        h = self.ccs[int(s[2:])] if s.startswith("cc") else self.sem[s]
                        ins.then_inc(h, k)
            return run
        block.tensor(runner("pe"))
        block.vector(runner("dve"))
        block.sync(runner("sp"))
        block.scalar(runner("sc"))
        block.gpsimd(runner("gp"))


def build():
    nc = bass.Bass("TRN2", target_bir_lowering=False)
    dt = nc.dram_tensor
    x_in = dt("x", [TOK, D], F32, kind="ExternalInput").ap()
    y_out = dt("y", [TOK, D], F32, kind="ExternalOutput").ap()

    IN = {}
    for nm, shp in [
        ("cos_t", [128, S]), ("sin_t", [128, S]), ("Ltri", [128, 128]),
        ("halfones", [128, 1]), ("ident", [128, 128]), ("cmall", [128, 2048]),
        ("wf1", [D, 128]), ("wg1", [D, 128]),
        ("ln1w", [128, D]), ("ln1b", [128, D]), ("ln2w", [128, D]), ("ln2b", [128, D]),
        ("normw", [128, DH]), ("wgate", [128, 64]),
        ("wqa", [D, 512]), ("wka", [D, 512]), ("wva", [D, 512]), ("wwa", [D, 512]),
        ("cqa", [128, 16]), ("cka", [128, 16]), ("cva", [128, 16]),
        ("wf2s", [128, 512]), ("wg2s", [128, 512]), ("bg2s", [128, 512]), ("wbs", [D, 4]),
        ("woa_s", [512, D]),
        ("wqb", [D, 512]), ("wkb", [D, 512]), ("wvb", [D, 512]),
        ("bqb", [128, 4]), ("bkb", [128, 4]), ("bvb", [128, 512]),
        ("wob_s", [512, D]), ("obb_bc", [128, D]),
        ("wqf", [D, 512]), ("wkf", [D, 512]), ("wvf", [D, 512]),
        ("bqf", [128, 4]), ("bkf", [128, 4]), ("bvf", [128, 512]),
        ("wof_s", [512, D]), ("bof_bc", [128, D]),
        ("we1e", [D, D]), ("we2e", [D, D]), ("be1e", [128, 16]), ("be2e", [1, D]),
        ("onehots", [16, 4]),
        ("consts_in", [128, 16]), ("ones_row", [1, 512]), ("zeros_row", [1, 512]),
        ("zeros128", [128, 128]), ("vaug_ones", [128, 32 * 130]),
    ]:
        IN[nm] = dt(nm, shp, F32, kind="ExternalInput").ap()

    x_dram = dt("x_dram", [TOK, D], F32)
    ag_in = dt("ag_in", [TOK, D], BF16)
    ag_out = dt("ag_out", [4 * TOK, D], BF16)
    qT_d = dt("qT_d", [512, S], BF16)
    kT_d = dt("kT_d", [512, S], BF16)
    vT_d = dt("vT_d", [512, S], BF16)
    wT_d = dt("wT_d", [512, S], BF16)
    zT_d = dt("zT_d", [512, S], BF16)
    v_tm = dt("v_tm", [S, 512], BF16)
    qcT_d = dt("qcT_d", [512, S], BF16)
    kcT_d = dt("kcT_d", [512, S], BF16)
    vcT_d = dt("vcT_d", [512, S], BF16)
    wcT_d = dt("wcT_d", [512, S], BF16)
    g_d = dt("g_d", [512, S], F32)
    beta_d = dt("beta_d", [4, S], BF16)
    gate_d = dt("gate_d", [S, 512], BF16)
    hm_d = dt("hm_d", [D, S], BF16)
    rs_in = dt("rs_in", [S, D], F32)
    rs_out = dt("rs_out", [TOK, D], F32)
    rs_in_c = [dt(f"rs_in_c{n}", [S, 512], F32) for n in range(4)]
    rs_out_c = [dt(f"rs_out_c{n}", [TOK, 512], F32) for n in range(4)]
    combT_d = dt("combT_d", [4, TOK], F32)
    oT_dbg = dt("oT_dbg", [512, S], BF16)
    sc_dbg = dt("sc_dbg", [128, 32 * 128], F32)
    combT_g = dt("combT_g", [16, TOK], F32)

    stk = ExitStack()
    sb = lambda name, shp, dtp: stk.enter_context(nc.sbuf_tensor(name, shp, dtp))
    P = [stk.enter_context(nc.psum_tensor(f"P{i}", [128, 512], F32)) for i in range(7)]
    PT = stk.enter_context(nc.psum_tensor("PT", [128, 1024], BF16))

    idf = sb("idf", [128, 128], F32)
    idb = sb("idb", [128, 128], BF16)
    ltri = sb("ltri", [128, 128], F32)
    mst = sb("mst", [128, 128], BF16)
    hones = sb("hones", [128, 1], F32)
    cosb = sb("cosb", [128, S], BF16)
    sinb = sb("sinb", [128, S], BF16)
    cmb = sb("cmb", [128, 2048], BF16)
    nwb = sb("nwb", [128, DH], BF16)
    ones_cb = sb("ones_cb", [128, 1], BF16)
    ones_cf = sb("ones_cf", [128, 1], F32)
    ones_r = sb("ones_r", [1, 512], F32)
    ones_rb = sb("ones_rb", [1, 128], BF16)
    ln1w_b = sb("ln1w_b", [128, D], BF16)
    ln1b_b = sb("ln1b_b", [128, D], BF16)
    cq_s = sb("cq_s", [128, 16], F32)
    ck_s = sb("ck_s", [128, 16], F32)
    cv_s = sb("cv_s", [128, 16], F32)
    be1_s = sb("be1_s", [128, 16], F32)
    wgate_s = sb("wgate_s", [128, 64], F32)
    onehot_s = sb("onehot_s", [16, 4], F32)
    bias_c = {nm: sb(nm + "_c", [128, 4], F32) for nm in ("bqb", "bkb", "bqf", "bkf")}
    bias_b = {nm: sb(nm + "_b", [128, 512], F32) for nm in ("bvb", "bvf", "bg2s")}
    rbias = {nm: sb(nm + "_rb", [128, D], F32) for nm in ("obb_bc", "bof_bc")}
    r0_t = sb("r0_t", [128, NT], F32)
    nm0_t = sb("nm0_t", [128, NT], F32)
    rr_t = sb("rr_t", [128, NT], F32)
    nmr_t = sb("nmr_t", [128, NT], F32)
    sqd = sb("sqd", [128, D], BF16)
    cst = sb("cst", [128, 16], F32)
    zcol = sb("zcol", [128, 128], F32)
    st1 = sb("st1", [128, NT], F32)
    st2 = sb("st2", [128, NT], F32)
    st3 = sb("st3", [128, NT], F32)

    blockstk = ExitStack()
    block = blockstk.enter_context(nc.Block())
    p = Prog(nc)

    for dst, src in [(idf[:, :], IN["ident"]), (ltri[:, :], IN["Ltri"]),
                     (hones[:, :], IN["halfones"]), (cq_s[:, :], IN["cqa"]),
                     (ck_s[:, :], IN["cka"]), (cv_s[:, :], IN["cva"]),
                     (be1_s[:, :], IN["be1e"]), (wgate_s[:, :], IN["wgate"]),
                     (onehot_s[:, :], IN["onehots"])]:
        p.dma("sp", dst, src)
    for nm in bias_c:
        p.dma("sp", bias_c[nm][:, :], IN[nm])
    for nm in bias_b:
        p.dma("gq", bias_b[nm][:, :], IN[nm])
    for nm in rbias:
        p.dma("gq", rbias[nm][:, :], IN[nm])
    for dst, src in [(idb[:, :], IN["ident"]), (mst[:, :], IN["Ltri"]),
                     (cosb[:, :], IN["cos_t"]), (sinb[:, :], IN["sin_t"]),
                     (cmb[:, :], IN["cmall"]), (nwb[:, :], IN["normw"]),
                     (ln1w_b[:, :], IN["ln1w"]), (ln1b_b[:, :], IN["ln1b"])]:
        p.dma("gq", dst, src)
    p.dma("sp", cst[:, :], IN["consts_in"])
    p.dma("sp", ones_r[:, :], IN["ones_row"])
    p.dma("sp", zcol[:, :], IN["zeros128"])
    p.dma("gq", ones_cb[:, :], IN["consts_in"][:, 6:7], allow_slow_non_contiguous=True)
    p.dma("sp", ones_cf[:, :], IN["consts_in"][:, 6:7], allow_slow_non_contiguous=True)
    SET = p.dma("gq", ones_rb[:, :], IN["ones_row"][0:1, 0:128])

    uid = [0]

    def fresh(prefix):
        uid[0] += 1
        return f"{prefix}_{uid[0]}"

    def cast_w(dst, src_ap, cols, nk=16, nsplit=4, deps=()):
        kper = nk // nsplit
        t = None
        for j in range(nsplit):
            t = p.dma(
                "gq",
                dst[:, cols * kper * j:cols * kper * (j + 1)].rearrange("p (k c) -> p k c", c=cols),
                src_ap[128 * kper * j:128 * kper * (j + 1), :].rearrange("(k p) c -> p k c", p=128),
                deps=deps if j == 0 else ())
        return t

    def ln_phase(x_src, mode, ag_free_dep, entry=None):
        with ExitStack() as ph:
            xs = [ph.enter_context(nc.sbuf_tensor(fresh("lnx"), [128, D], F32))
                  for _ in range(NT)]
            lnb = ph.enter_context(nc.sbuf_tensor(fresh("lnb"), [128, D], BF16))
            ld, vt, at, ok = [], [], [], []
            for i in range(NT):
                ld.append(p.dma("sp", xs[i][:, :], x_src[128 * i:128 * (i + 1), :],
                                deps=[entry] if i == 0 else ()))
            for i in range(NT):
                vt.append(p.dve(lambda v, i=i: v.reduce_sum(st1[:, i:i + 1], xs[i][:, :], AX.X),
                                deps=[ld[i]]))
                at.append(p.act(lambda s, i=i: s.activation(
                    sqd[:, :], xs[i][:, :], AF.Square, accum_out=st2[:, i:i + 1]),
                    deps=[ld[i]]))
            for i in range(NT):
                p.dve(lambda v, i=i: v.tensor_scalar(st1[:, i:i + 1], st1[:, i:i + 1],
                                                     1.0 / D, None, ALU.mult), deps=[vt[i]])
                p.dve(lambda v, i=i: v.tensor_scalar(st2[:, i:i + 1], st2[:, i:i + 1],
                                                     1.0 / D, None, ALU.mult), deps=[at[i]])
                p.dve(lambda v, i=i: v.scalar_tensor_tensor(
                    st2[:, i:i + 1], st1[:, i:i + 1], st1[:, i:i + 1], st2[:, i:i + 1],
                    ALU.mult, ALU.subtract))
                t = p.dve(lambda v, i=i: v.tensor_scalar(st2[:, i:i + 1], st2[:, i:i + 1],
                                                         -1.0, None, ALU.mult))
                t = p.act(lambda s, i=i: s.activation(st3[:, i:i + 1], st2[:, i:i + 1],
                                                      AF.Ln, bias=cst[:, 0:1]), deps=[t])
                t = p.act(lambda s, i=i: s.activation(r0_t[:, i:i + 1], st3[:, i:i + 1],
                                                      AF.Exp, scale=-0.5))
                t = p.dve(lambda v, i=i: v.tensor_scalar(
                    nm0_t[:, i:i + 1], st1[:, i:i + 1], r0_t[:, i:i + 1], -1.0,
                    ALU.mult, ALU.mult), deps=[t])
                if mode == "ffn":
                    t = p.dve(lambda v, i=i: v.tensor_scalar(
                        st3[:, i:i + 1], r0_t[:, i:i + 1], r0_t[:, i:i + 1], -LN_EPS,
                        ALU.mult, ALU.mult))
                    t = p.dve(lambda v, i=i: v.tensor_scalar(st3[:, i:i + 1], st3[:, i:i + 1],
                                                              1.0 + LN_EPS, None, ALU.add), deps=[t])
                    t = p.act(lambda s, i=i: s.activation(st3[:, i:i + 1], st3[:, i:i + 1],
                                                          AF.Ln), deps=[t])
                    t = p.act(lambda s, i=i: s.activation(st3[:, i:i + 1], st3[:, i:i + 1],
                                                          AF.Exp, scale=-0.5))
                    t = p.dve(lambda v, i=i: v.tensor_tensor(
                        rr_t[:, i:i + 1], r0_t[:, i:i + 1], st3[:, i:i + 1], ALU.mult), deps=[t])
                    t = p.dve(lambda v, i=i: v.tensor_scalar(
                        nmr_t[:, i:i + 1], st1[:, i:i + 1], rr_t[:, i:i + 1], -1.0,
                        ALU.mult, ALU.mult))
                ok.append(t)
            sto = ag_free_dep
            for i in range(NT):
                if mode == "ffn":
                    t = p.dve(lambda v, i=i: v.tensor_scalar(
                        lnb[:, :], xs[i][:, :], rr_t[:, i:i + 1], nmr_t[:, i:i + 1],
                        ALU.mult, ALU.add), deps=[ok[i], sto])
                    t = p.dve(lambda v: v.tensor_tensor(lnb[:, :], lnb[:, :], ln1w_b[:, :], ALU.mult))
                    t = p.dve(lambda v: v.tensor_tensor(lnb[:, :], lnb[:, :], ln1b_b[:, :], ALU.add))
                else:
                    t = p.dve(lambda v, i=i: v.tensor_scalar(
                        lnb[:, :], xs[i][:, :], r0_t[:, i:i + 1], nm0_t[:, i:i + 1],
                        ALU.mult, ALU.add), deps=[ok[i], sto])
                sto = p.dma("sp", ag_in.ap()[128 * i:128 * (i + 1), :], lnb[:, :], deps=[t])
            return sto, ld[-1]

    def load_rhsT(rhsT, r, h, ccdep, extra=()):
        t = None
        base = TOK * r + 512 * h
        for k in range(16):
            t = p.tload(rhsT[:, 512 * k:512 * (k + 1)],
                        ag_out.ap()[base:base + 512, 128 * k:128 * (k + 1)],
                        deps=[ccdep] + list(extra))
        return t

    def proj_qkv(wq_ap, wk_ap, wv_ap, bqn, bkn, bvn, ccdep, seed):
        with ExitStack() as ph:
            psb = lambda shp, d_: ph.enter_context(nc.sbuf_tensor(fresh("pq"), shp, d_))
            wsb = {k: psb([128, 8192], BF16) for k in "qkv"}
            rhsT = psb([128, 8192], BF16)
            q0 = psb([128, 512], BF16)
            rot = psb([128, 512], BF16)
            t1b = psb([128, 512], BF16)
            stg = [psb([128, 512], BF16) for _ in range(2)]
            wt = {"q": cast_w(wsb["q"], wq_ap, 512, deps=[seed]),
                  "k": cast_w(wsb["k"], wk_ap, 512),
                  "v": cast_w(wsb["v"], wv_ap, 512)}
            bcol = {"q": bias_c[bqn], "k": bias_c[bkn]}
            bvbc = bias_b[bvn]
            prev = [None, None]
            laststore = seed
            last_rot = None
            pi = 0
            for r in range(4):
                for hh in range(2):
                    rt = load_rhsT(rhsT, r, hh, ccdep, extra=[laststore])
                    cols = slice(TOK * r + 512 * hh, TOK * r + 512 * (hh + 1))
                    for wkey in ("q", "k"):
                        outd = qT_d if wkey == "q" else kT_d
                        for m in range(4):
                            slot = pi % 2
                            bank = P[slot]
                            pi += 1

                            def mmf(t, wkey=wkey, m=m, bank=bank):
                                ins = None
                                for k in range(16):
                                    ins = t.matmul(bank[:, :],
                                                   wsb[wkey][:, 512 * k + 128 * m:512 * k + 128 * (m + 1)],
                                                   rhsT[:, 512 * k:512 * (k + 1)],
                                                   start=(k == 0), stop=(k == 15))
                                return ins
                            mt = p.pe(mmf, deps=[rt, wt[wkey], prev[slot]])
                            t = p.dve(lambda v, bank=bank, wkey=wkey, m=m: v.tensor_scalar(
                                q0[:, :], bank[:, :], bcol[wkey][:, m:m + 1], None, ALU.add),
                                deps=[mt, last_rot])
                            prev[slot] = t
                            t2 = p.pool(lambda g: g.tensor_copy(rot[0:64, :], q0[64:128, :]),
                                        deps=[t])
                            t2 = p.pool(lambda g: g.tensor_copy(rot[64:128, :], q0[0:64, :]))
                            last_rot = t2
                            qsc = SCALE_A if wkey == "q" else 1.0
                            t3 = p.dve(lambda v, cols=cols, qsc=qsc: v.scalar_tensor_tensor(
                                t1b[:, :], q0[:, :], qsc, cosb[:, cols], ALU.mult, ALU.mult),
                                deps=[t])
                            sg = stg[slot]
                            t4 = p.dve(lambda v, cols=cols, sg=sg, qsc=qsc: v.scalar_tensor_tensor(
                                sg[:, :], rot[:, :], qsc, sinb[:, cols], ALU.mult, ALU.mult),
                                deps=[t2, t3, laststore])
                            t4 = p.dve(lambda v, sg=sg: v.tensor_tensor(sg[:, :], sg[:, :],
                                                                        t1b[:, :], ALU.add))
                            laststore = p.dma("sp", outd.ap()[128 * m:128 * (m + 1), cols],
                                              sg[:, :], deps=[t4])
                    for j in range(4):
                        slot = pi % 2
                        bank = P[slot]
                        pi += 1

                        def mmv(t, j=j, bank=bank):
                            ins = None
                            for k in range(16):
                                ins = t.matmul(bank[:, :],
                                               rhsT[:, 512 * k + 128 * j:512 * k + 128 * (j + 1)],
                                               wsb["v"][:, 512 * k:512 * (k + 1)],
                                               start=(k == 0), stop=(k == 15))
                            return ins
                        mt = p.pe(mmv, deps=[rt, wt["v"], prev[slot]])
                        sg = stg[slot]
                        t = p.dve(lambda v, bank=bank, sg=sg: v.tensor_tensor(
                            sg[:, :], bank[:, :], bvbc[:, :], ALU.add),
                            deps=[mt, laststore])
                        prev[slot] = t
                        row0 = TOK * r + 512 * hh + 128 * j
                        laststore = p.dma("sp", v_tm.ap()[row0:row0 + 128, :],
                                          sg[:, :], deps=[t])
            return laststore

    def attn_inner(causal, oT, proj_dep):
        with ExitStack() as ph:
            psb = lambda shp, d_: ph.enter_context(nc.sbuf_tensor(fresh("ai"), shp, d_))
            qTh = psb([128, S], BF16)
            kTh = psb([128, S], BF16)
            vaug = psb([128, 32 * 130], BF16)
            pt = [psb([128, 512], BF16) for _ in range(3)]
            ostg = psb([128, 128], BF16)
            recs = psb([128, 1], F32)
            otok = None
            vload = p.dma("gq", vaug[:, :], IN["vaug_ones"], deps=[proj_dep])
            prev_bh = [proj_dep]
            for bh in range(H):
                rows = slice(128 * bh, 128 * (bh + 1))
                lq = p.dma("sp", qTh[:, :], qT_d.ap()[rows, :], deps=prev_bh)
                lk = p.dma("sp", kTh[:, :], kT_d.ap()[rows, :])
                lv = p.dma("sp", vaug[:, :].rearrange("p (k j) -> p k j", j=130)[:, :, 0:128],
                           v_tm.ap()[:, 128 * bh:128 * (bh + 1)].rearrange("(k p) j -> p k j", p=128),
                           deps=[vload] + prev_bh)
                exp_tok = [None, None]
                ptok = [None, None, None]
                ob = [P[3], P[4], P[5], P[6]]
                oev = [None, None, None, None]
                for qc in range(8):
                    nkt = 4 * (qc + 1) if causal else 32
                    acc = [None] * 4
                    for kt in range(nkt):
                        slot = kt % 2
                        bank = P[slot]

                        def smm(t, kt=kt, qc=qc, bank=bank):
                            return t.matmul(bank[:, :], kTh[:, 128 * kt:128 * (kt + 1)],
                                            qTh[:, 512 * qc:512 * (qc + 1)],
                                            start=True, stop=True)
                        st_ = p.pe(smm, deps=[lq, lk, exp_tok[slot]])
                        pti = kt % 3

                        def ex(s_, bank=bank, pti=pti):
                            return s_.activation(pt[pti][:, :], bank[:, :], AF.Exp, bias=cst[:, 5:6])
                        et = p.act(ex, deps=[st_, ptok[pti]])
                        exp_tok[slot] = et
                        ready = et
                        if causal and kt >= 4 * qc:
                            mi = (128 * kt - 512 * qc) // 128

                            def mk(v, pti=pti, mi=mi):
                                return v.tensor_tensor(pt[pti][:, :], pt[pti][:, :],
                                                       cmb[:, 512 * mi:512 * (mi + 1)], ALU.mult)
                            ready = p.dve(mk, deps=[et])
                        last = None
                        for qt in range(4):
                            def pv(t, pti=pti, qt=qt, kt=kt, first=(acc[qt] is None)):
                                return t.matmul(ob[qt][:, 0:129],
                                                pt[pti][:, 128 * qt:128 * (qt + 1)],
                                                vaug[:, 130 * kt:130 * kt + 129],
                                                start=first, stop=False)
                            last = p.pe(pv, deps=[ready, lv, acc[qt] or oev[qt]])
                            acc[qt] = last
                        ptok[pti] = last
                    for qt in range(4):
                        t = p.dve(lambda v, qt=qt: v.reciprocal(recs[:, 0:1], ob[qt][:, 128:129]),
                                  deps=[acc[qt], otok])
                        t = p.dve(lambda v, qt=qt: v.tensor_scalar(
                            ostg[:, :], ob[qt][:, 0:128], recs[:, 0:1], None, ALU.mult))
                        tt = p.pe(lambda t_: t_.transpose(PT[:, 0:128], ostg[:, :], idb[:, :]),
                                  deps=[t, otok])
                        otok = p.dve(lambda v, bh=bh, qc=qc, qt=qt: v.tensor_copy(
                            oT[bh][:, 512 * qc + 128 * qt:512 * qc + 128 * (qt + 1)],
                            PT[:, 0:128]), deps=[tt])
                        oev[qt] = otok
                prev_bh = [otok]
            return otok

    def out_proj_rs(oTbig, wo_nm, dep, rs_free_ccs, rsout_free):
        with ExitStack() as ph:
            psb = lambda shp, d_: ph.enter_context(nc.sbuf_tensor(fresh("op"), shp, d_))
            wo_sb = psb([128, 4 * D], BF16)
            stg = [psb([128, 512], F32) for _ in range(2)]
            wt = cast_w(wo_sb, IN[wo_nm], D, nk=4, nsplit=1, deps=[dep])
            prev = [None, None]
            ccs = []
            for n in range(4):
                sto = rs_free_ccs[n]
                for tt in range(32):
                    slot = (n * 32 + tt) % 2
                    bank = P[slot]

                    def mmf(t, tt=tt, n=n, bank=bank):
                        ins = None
                        for k in range(4):
                            ins = t.matmul(bank[:, :],
                                           oTbig[:, S * k + 128 * tt:S * k + 128 * (tt + 1)],
                                           wo_sb[:, D * k + 512 * n:D * k + 512 * (n + 1)],
                                           start=(k == 0), stop=(k == 3))
                        return ins
                    mt = p.pe(mmf, deps=[dep, wt, prev[slot]])
                    sg = stg[slot]
                    t = p.dve(lambda v, bank=bank, sg=sg: v.tensor_copy(sg[:, :], bank[:, :]),
                              deps=[mt, sto])
                    prev[slot] = t
                    sto = p.dma("sp", rs_in_c[n].ap()[128 * tt:128 * (tt + 1), :],
                                sg[:, :], deps=[t])
                ccs.append(p.coll("ReduceScatter", ALU.add, [rs_in_c[n].ap().opt()],
                                  [rs_out_c[n].ap().opt()],
                                  deps=[sto, rsout_free[n]]))
            return ccs

    def residual(x_src, cc, mode, h3buf=None, final=False, bias=None):
        chunked = isinstance(cc, list)
        with ExitStack() as ph:
            psb = lambda shp, d_: ph.enter_context(nc.sbuf_tensor(fresh("rs"), shp, d_))
            xts = [psb([128, D], F32) for _ in range(2)]
            ats = [psb([128, D], F32) for _ in range(2)]
            dones = [None, None]
            cfree = [None] * 4
            last = None
            for i in range(NT):
                s = i % 2
                xt, at = xts[s], ats[s]
                lx = p.dma("sp", xt[:, :], x_src[128 * i:128 * (i + 1), :], deps=[dones[s]])
                if chunked:
                    for n in range(4):
                        la = p.dma("sp", at[:, 512 * n:512 * (n + 1)],
                                   rs_out_c[n].ap()[128 * i:128 * (i + 1), :],
                                   deps=[cc[n], dones[s]])
                        cfree[n] = la
                else:
                    la = p.dma("sp", at[:, :], rs_out.ap()[128 * i:128 * (i + 1), :],
                               deps=[cc, dones[s]])
                if mode == "plain":
                    t = p.dve(lambda v, xt=xt, at=at: v.tensor_tensor(
                        xt[:, :], xt[:, :], at[:, :], ALU.add), deps=[lx, la])
                    if bias is not None:
                        t = p.dve(lambda v, xt=xt: v.tensor_tensor(xt[:, :], xt[:, :], bias[:, :],
                                                                   ALU.add))
                else:
                    t = p.dve(lambda v, i=i, xt=xt, at=at: v.tensor_scalar(
                        h3buf[i][:, :], xt[:, :], r0_t[:, i:i + 1], nm0_t[:, i:i + 1],
                        ALU.mult, ALU.add), deps=[lx, la])
                    t = p.dve(lambda v, i=i, at=at: v.tensor_tensor(
                        h3buf[i][:, :], h3buf[i][:, :], at[:, :], ALU.add))
                    if bias is not None:
                        t = p.dve(lambda v, i=i: v.tensor_tensor(
                            h3buf[i][:, :], h3buf[i][:, :], bias[:, :], ALU.add))
                    t = p.dve(lambda v, i=i, xt=xt: v.tensor_tensor(
                        xt[:, :], xt[:, :], h3buf[i][:, :], ALU.add))
                last = p.dma("sp", x_dram.ap()[128 * i:128 * (i + 1), :], xt[:, :], deps=[t])
                if final:
                    last = p.dma("sp", y_out[128 * i:128 * (i + 1), :], xt[:, :])
                dones[s] = last
            return last, (cfree if chunked else last)

    def proj_a(ccdep, join):
        with ExitStack() as ph:
            psb = lambda shp, d_: ph.enter_context(nc.sbuf_tensor(fresh("pa"), shp, d_))
            w4 = {nm: psb([128, 8192], BF16) for nm in ("q", "k", "v", "w")}
            wf1_sb = psb([128, 2048], BF16)
            wg1_sb = psb([128, 2048], BF16)
            wf2_sb = psb([128, 512], BF16)
            wg2_sb = psb([128, 512], BF16)
            wb_sb = psb([128, 64], BF16)
            rhsT = psb([128, 8192], BF16)
            z1_bf = psb([128, S], BF16)
            g1_bf = psb([128, S], BF16)
            stg = [psb([128, 512], BF16) for _ in range(2)]
            gbs2 = [psb([128, 512], F32) for _ in range(2)]
            wt = {}
            wt["q"] = cast_w(w4["q"], IN["wqa"], 512, deps=[join])
            wt["k"] = cast_w(w4["k"], IN["wka"], 512)
            wt["v"] = cast_w(w4["v"], IN["wva"], 512)
            wt["w"] = cast_w(w4["w"], IN["wwa"], 512)
            wt["f1"] = cast_w(wf1_sb, IN["wf1"], 128)
            wt["g1"] = cast_w(wg1_sb, IN["wg1"], 128)
            wt["b"] = cast_w(wb_sb, IN["wbs"], 4)
            p.dma("gq", wf2_sb[:, :], IN["wf2s"])
            wt["small"] = p.dma("gq", wg2_sb[:, :], IN["wg2s"])
            prev = [None, None]
            sto = join
            zt = None
            pi = 0
            for r in range(4):
                for hh in range(2):
                    rt = load_rhsT(rhsT, r, hh, ccdep, extra=[sto])
                    cols = slice(TOK * r + 512 * hh, TOK * r + 512 * (hh + 1))
                    for wkey, outd in (("q", qT_d), ("k", kT_d), ("v", vT_d), ("w", wT_d)):
                        for m in range(4):
                            slot = pi % 2
                            bank = P[slot]
                            pi += 1

                            def mmf(t, wkey=wkey, m=m, bank=bank):
                                ins = None
                                for k in range(16):
                                    ins = t.matmul(bank[:, :],
                                                   w4[wkey][:, 512 * k + 128 * m:512 * k + 128 * (m + 1)],
                                                   rhsT[:, 512 * k:512 * (k + 1)],
                                                   start=(k == 0), stop=(k == 15))
                                return ins
                            mt = p.pe(mmf, deps=[rt, wt[wkey], prev[slot]])
                            sg = stg[slot]
                            t = p.dve(lambda v, bank=bank, sg=sg: v.tensor_copy(sg[:, :], bank[:, :]),
                                      deps=[mt, sto])
                            prev[slot] = t
                            sto = p.dma("sp", outd.ap()[128 * m:128 * (m + 1), cols],
                                        sg[:, :], deps=[t])
                    for wsb1, dst in ((wf1_sb, z1_bf), (wg1_sb, g1_bf)):
                        slot = pi % 2
                        bank = P[slot]
                        pi += 1

                        def mmf2(t, wsb1=wsb1, bank=bank):
                            ins = None
                            for k in range(16):
                                ins = t.matmul(bank[:, :], wsb1[:, 128 * k:128 * (k + 1)],
                                               rhsT[:, 512 * k:512 * (k + 1)],
                                               start=(k == 0), stop=(k == 15))
                            return ins
                        mt = p.pe(mmf2, deps=[rt, wt["f1"], wt["g1"], prev[slot]])
                        zt = p.dve(lambda v, bank=bank, dst=dst, cols=cols: v.tensor_copy(
                            dst[:, cols], bank[:, :]), deps=[mt])
                        prev[slot] = zt
                    slot = pi % 2
                    bank = P[slot]
                    pi += 1

                    def mmb(t, bank=bank):
                        ins = None
                        for k in range(16):
                            ins = t.matmul(bank[0:4, :], wb_sb[:, 4 * k:4 * (k + 1)],
                                           rhsT[:, 512 * k:512 * (k + 1)],
                                           start=(k == 0), stop=(k == 15))
                        return ins
                    mt = p.pe(mmb, deps=[rt, wt["b"], prev[slot]])
                    sg = stg[slot]
                    t = p.act(lambda s_, bank=bank, sg=sg: s_.activation(
                        sg[0:4, :], bank[0:4, :], AF.Sigmoid), deps=[mt, sto])
                    prev[slot] = t
                    sto = p.dma("sp", beta_d.ap()[0:4, cols], sg[0:4, :], deps=[t])
            for m in range(4):
                for n in range(8):
                    slot = n % 2
                    bank = P[slot]

                    def mmz(t, m=m, n=n, bank=bank):
                        return t.matmul(bank[:, :], wf2_sb[:, 128 * m:128 * (m + 1)],
                                        z1_bf[:, 512 * n:512 * (n + 1)], start=True, stop=True)
                    mt = p.pe(mmz, deps=[zt, wt["small"], prev[slot]])
                    sg = stg[slot]
                    t = p.dve(lambda v, bank=bank, sg=sg: v.tensor_copy(sg[:, :], bank[:, :]),
                              deps=[mt, sto])
                    prev[slot] = t
                    sto = p.dma("sp", zT_d.ap()[128 * m:128 * (m + 1), 512 * n:512 * (n + 1)],
                                sg[:, :], deps=[t])
            for tt in range(32):
                slot = tt % 2
                bank = P[slot]

                def mmg(t, tt=tt, bank=bank):
                    return t.matmul(bank[:, :], g1_bf[:, 128 * tt:128 * (tt + 1)], wg2_sb[:, :],
                                    start=True, stop=True)
                mt = p.pe(mmg, deps=[zt, wt["small"], prev[slot]])
                sg = stg[slot]
                gb = gbs2[slot]
                t = p.dve(lambda v, bank=bank, gb=gb: v.tensor_tensor(
                    gb[:, :], bank[:, :], bias_b["bg2s"][:, :], ALU.add), deps=[mt])
                t = p.act(lambda s_, sg=sg, gb=gb: s_.activation(
                    sg[:, :], gb[:, :], AF.Sigmoid), deps=[t, sto])
                prev[slot] = t
                sto = p.dma("sp", gate_d.ap()[128 * tt:128 * (tt + 1), :], sg[:, :], deps=[t])
            return sto

    def conv_phase(proj_done):
        with ExitStack() as ph:
            psb = lambda shp, d_: ph.enter_context(nc.sbuf_tensor(fresh("cv"), shp, d_))
            sets = [dict(xpad=psb([128, S + 4], BF16), cacc=psb([128, S], BF16),
                         csc=psb([128, S], BF16)) for _ in range(2)]
            setdep = [proj_done, proj_done]
            sq2 = psb([128, S], BF16)
            grow = psb([128, S], F32)
            gtmp = psb([128, S], F32)
            lrow = psb([1, S], F32)
            wdep = proj_done
            gdep = proj_done
            idx = 0
            for bh in range(H):
                rows = slice(128 * bh, 128 * (bh + 1))
                for src_d, out_d, cw in ((qT_d, qcT_d, cq_s), (kT_d, kcT_d, ck_s),
                                         (vT_d, vcT_d, cv_s), (wT_d, wcT_d, cv_s)):
                    s_i = idx % 2
                    idx += 1
                    xpad, cacc, csc = (sets[s_i]["xpad"], sets[s_i]["cacc"],
                                       sets[s_i]["csc"])
                    ld1 = p.dma("sp", xpad[:, 3:S + 3], src_d.ap()[rows, :],
                                deps=[setdep[s_i]])
                    ld2 = p.dma("gq", xpad[:, 0:3], IN["zeros128"][:, 0:3],
                                deps=[setdep[s_i]])
                    t = p.dve(lambda v, cw=cw, bh=bh, cacc=cacc, xpad=xpad: v.tensor_scalar(
                        cacc[:, :], xpad[:, 0:S], cw[:, 4 * bh:4 * bh + 1], None, ALU.mult),
                        deps=[ld1, ld2])
                    for i in (1, 2, 3):
                        t = p.dve(lambda v, cw=cw, bh=bh, i=i, cacc=cacc, xpad=xpad:
                                  v.scalar_tensor_tensor(
                                      cacc[:, :], xpad[:, i:S + i],
                                      cw[:, 4 * bh + i:4 * bh + i + 1],
                                      cacc[:, :], ALU.mult, ALU.add))
                    t = p.act(lambda s_, csc=csc, cacc=cacc: s_.activation(
                        csc[:, :], cacc[:, :], AF.Silu), deps=[t])
                    if out_d is not wcT_d:
                        sto = p.dma("sp", out_d.ap()[rows, :], csc[:, :], deps=[t])
                        setdep[s_i] = sto
                    else:
                        tq2 = p.act(lambda s_, csc=csc: s_.activation(
                            sq2[:, :], csc[:, :], AF.Square), deps=[t, wdep])
                        rowt = wdep
                        for n2 in range(8):
                            slot = n2 % 2
                            bank = P[slot]
                            mt = p.pe(lambda t_, n2=n2, bank=bank: t_.matmul(
                                bank[0:1, :], ones_cb[:, 0:1], sq2[:, 512 * n2:512 * (n2 + 1)],
                                start=True, stop=True), deps=[tq2, rowt])
                            rowt = p.dve(lambda v, n2=n2, bank=bank: v.tensor_copy(
                                lrow[0:1, 512 * n2:512 * (n2 + 1)], bank[0:1, :]), deps=[mt])
                        t = p.dve(lambda v: v.tensor_scalar(lrow[0:1, :], lrow[0:1, :],
                                                            1e-6, None, ALU.add), deps=[rowt])
                        t = p.act(lambda s_: s_.activation(lrow[0:1, :], lrow[0:1, :],
                                                           AF.Ln), deps=[t])
                        t = p.act(lambda s_: s_.activation(lrow[0:1, :], lrow[0:1, :],
                                                           AF.Exp, scale=-0.5))
                        mul_t = t
                        for n2 in range(8):
                            slot = n2 % 2
                            bank = P[slot]
                            mt = p.pe(lambda t_, n2=n2, bank=bank: t_.matmul(
                                bank[:, :], ones_r[0:1, 0:128], lrow[0:1, 512 * n2:512 * (n2 + 1)],
                                start=True, stop=True), deps=[t, mul_t])
                            mul_t = p.dve(lambda v, n2=n2, bank=bank, csc=csc: v.tensor_tensor(
                                csc[:, 512 * n2:512 * (n2 + 1)], csc[:, 512 * n2:512 * (n2 + 1)],
                                bank[:, :], ALU.mult), deps=[mt])
                        wdep = mul_t
                        sto = p.dma("sp", wcT_d.ap()[rows, :], csc[:, :], deps=[mul_t])
                        setdep[s_i] = sto
                s_i = idx % 2
                idx += 1
                cacc = sets[s_i]["cacc"]
                ld = p.dma("sp", cacc[:, :], zT_d.ap()[rows, :], deps=[setdep[s_i]])
                t = p.act(lambda s_, cacc=cacc: s_.activation(grow[:, :], cacc[:, :], AF.Abs),
                          deps=[ld, gdep])
                t = p.act(lambda s_: s_.activation(grow[:, :], grow[:, :], AF.Exp, scale=-1.0))
                t = p.dve(lambda v: v.tensor_scalar(grow[:, :], grow[:, :], 1.0, None, ALU.add),
                          deps=[t])
                t = p.act(lambda s_: s_.activation(grow[:, :], grow[:, :], AF.Ln), deps=[t])
                t = p.dve(lambda v, cacc=cacc: v.tensor_scalar(
                    gtmp[:, :], cacc[:, :], 0.0, None, ALU.min), deps=[t])
                t = p.dve(lambda v: v.tensor_tensor(gtmp[:, :], gtmp[:, :], grow[:, :],
                                                    ALU.subtract))
                gdep = t
                sto = p.dma("sp", g_d.ap()[rows, :], gtmp[:, :], deps=[t])
                setdep[s_i] = sto
            join = p.dma("sp", zcol[0:1, 0:1], IN["zeros_row"][0:1, 0:1],
                         deps=[setdep[0], setdep[1]])
            return join

    scdbg_holder = [None]

    def scan_phase(conv_done, oT):
        with ExitStack() as ph:
            psb = lambda shp, d_: ph.enter_context(nc.sbuf_tensor(fresh("sc"), shp, d_))

            def mkbufs():
                b = {}
                b["qTh"] = psb([128, S], BF16)
                b["kTh"] = psb([128, S], BF16)
                b["wTh"] = psb([128, S], BF16)
                b["brow"] = psb([1, S], BF16)
                b["Sk"] = [psb([128, 128], F32) for _ in range(2)]
                b["Sv"] = [psb([128, 128], F32) for _ in range(2)]
                for nm in ("Skb Svb einv wbt uT uS abar pa a2 a2t a1t pbar "
                           "Kc Vc gatec og").split():
                    b[nm] = psb([128, 128], BF16)
                for nm in ("gmT gsm ltm lmT dtm e2s e1s e2m t1s t2s sps o1 o2 "
                           "lend lrel tsk tsk2 tsv sqh").split():
                    b[nm] = psb([128, 128], F32)
                for nm in "mrow erow trow".split():
                    b[nm] = psb([1, 128], F32)
                for nm in "bv1 nmx zr rz ratc ss1 rr1 ecol rcol tcol".split():
                    b[nm] = psb([128, 1], F32)
                b["st"] = None
                return b

            bodies = [mkbufs() for _ in range(2)]
            bank_free = {i: None for i in range(7)}
            ptfree = {"u": None, "a2": None, "og": None}
            glob = {"otok": None}

            def head_load(b, bh):
                rows = slice(128 * bh, 128 * (bh + 1))
                prev = [conv_done, b["st"], glob["otok"]]
                b["lq"] = p.dma("sp", b["qTh"][:, :], qcT_d.ap()[rows, :], deps=prev)
                b["lk"] = p.dma("sp", b["kTh"][:, :], kcT_d.ap()[rows, :])
                b["lw"] = p.dma("sp", b["wTh"][:, :], wcT_d.ap()[rows, :])
                b["lb"] = p.dma("sp", b["brow"][0:1, :], beta_d.ap()[bh:bh + 1, :])
                b["tq"] = p.dve(lambda v, b=b: v.tensor_scalar(
                    b["qTh"][:, :], b["qTh"][:, :], SCALE_A, None, ALU.mult),
                    deps=[b["lq"]])
                z0 = p.dma("sp", b["Sk"][0][:, :], IN["zeros128"], deps=prev)
                z0 = p.dma("sp", b["Sv"][0][:, :], IN["zeros128"])
                z0 = p.dma("gq", b["Skb"][:, :], IN["zeros128"], deps=prev)
                z0 = p.dma("gq", b["Svb"][:, :], IN["zeros128"])
                b["st"] = z0

            def body(b, bh, ch):
                rows = slice(128 * bh, 128 * (bh + 1))
                cs_ = slice(128 * ch, 128 * (ch + 1))
                cur, nxt = ch % 2, 1 - ch % 2
                st_upd = b["st"]
                qTh, kTh, wTh, brow = b["qTh"], b["kTh"], b["wTh"], b["brow"]
                Sk, Sv, Skb, Svb = b["Sk"], b["Sv"], b["Skb"], b["Svb"]
                gmT, gsm, ltm, lmT = b["gmT"], b["gsm"], b["ltm"], b["lmT"]
                mrow, erow, trow, dtm = b["mrow"], b["erow"], b["trow"], b["dtm"]
                e2s, e1s, e2m, bv1 = b["e2s"], b["e1s"], b["e2m"], b["bv1"]
                einv, wbt, uT, uS = b["einv"], b["wbt"], b["uT"], b["uS"]
                abar, t1s, t2s, sps = b["abar"], b["t1s"], b["t2s"], b["sps"]
                nmx, zr, rz, pa = b["nmx"], b["zr"], b["rz"], b["pa"]
                a2, a2t, a1t, ratc = b["a2"], b["a2t"], b["a1t"], b["ratc"]
                pbar, Kc, Vc, gatec = b["pbar"], b["Kc"], b["Vc"], b["gatec"]
                ss1, rr1, o1, o2, og = b["ss1"], b["rr1"], b["o1"], b["o2"], b["og"]
                lend, lrel, ecol, rcol = b["lend"], b["lrel"], b["ecol"], b["rcol"]
                tcol, tsk, tsk2, tsv, sqh = (b["tcol"], b["tsk"], b["tsk2"],
                                             b["tsv"], b["sqh"])
                lg = p.dma("sp", gmT[:, :], g_d.ap()[rows, cs_], deps=[st_upd])
                lkc = p.tload(Kc[:, :], kcT_d.ap()[rows, cs_], deps=[st_upd])
                lvc = p.tload(Vc[:, :], vcT_d.ap()[rows, cs_], deps=[st_upd])
                lgt = p.dma("sp", gatec[:, :],
                            gate_d.ap()[cs_, 128 * bh:128 * (bh + 1)],
                            deps=[st_upd, glob["otok"]])
                t = p.pe(lambda t_: t_.transpose(P[0][:, 0:128], gmT[:, :], idf[:, :]),
                         deps=[lg, bank_free[0]])
                tg = p.dve(lambda v: v.tensor_copy(gsm[:, :], P[0][:, 0:128]), deps=[t])
                t = p.pe(lambda t_: t_.matmul(P[1][:, 0:128], ltri[:, :], gsm[:, :],
                                              start=True, stop=True),
                         deps=[tg, bank_free[1]])
                tl1 = p.dve(lambda v: v.tensor_copy(ltm[:, :], P[1][:, 0:128]), deps=[t])
                t = p.pe(lambda t_: t_.matmul(P[2][:, 0:128], gsm[:, :], ltri[:, :],
                                              start=True, stop=True),
                         deps=[tg, bank_free[2]])
                tl2 = p.dve(lambda v: v.tensor_copy(lmT[:, :], P[2][:, 0:128]), deps=[t])
                t = p.pe(lambda t_: t_.matmul(P[3][0:1, 0:128], hones[:, 0:1], gsm[:, :],
                                              start=True, stop=True),
                         deps=[tg, bank_free[3]])
                t = p.pe(lambda t_: t_.matmul(P[3][0:1, 128:256], ones_cf[:, 0:1], gsm[:, :],
                                              start=True, stop=True))
                tr = p.dve(lambda v: v.tensor_copy(mrow[0:1, :], P[3][0:1, 0:128]), deps=[t])
                tr = p.dve(lambda v: v.tensor_copy(erow[0:1, :], P[3][0:1, 128:256]))
                t = p.pe(lambda t_: t_.matmul(P[4][:, 0:128], ones_r[0:1, 0:128], mrow[0:1, :],
                                              start=True, stop=True),
                         deps=[tr, bank_free[4]])
                t = p.dve(lambda v: v.tensor_tensor(dtm[:, :], ltm[:, :], P[4][:, 0:128],
                                                    ALU.subtract), deps=[t, tl1])
                bank_free[4] = t
                te2 = p.act(lambda s_: s_.activation(e2s[:, :], dtm[:, :], AF.Exp,
                                                     bias=cst[:, 4:5]), deps=[t])
                te1 = p.act(lambda s_: s_.activation(e1s[:, :], ltm[:, :], AF.Exp), deps=[tl1])
                te2m = p.dve(lambda v: v.tensor_scalar(e2m[:, :], e2s[:, :], EM44, None,
                                                       ALU.mult), deps=[te2])
                t = p.dve(lambda v: v.tensor_scalar(bv1[:, 0:1], lmT[:, 64:65], -SHIFT,
                                                    None, ALU.add), deps=[tl2])
                tei = p.act(lambda s_: s_.activation(einv[:, :], lmT[:, :], AF.Exp,
                                                     scale=-1.0, bias=bv1[:, 0:1]), deps=[t])
                t = p.pe(lambda t_, cs_=cs_: t_.matmul(P[6][:, 256:384], ones_rb[0:1, :],
                                                       brow[0:1, cs_], start=True, stop=True),
                         deps=[b["lb"], bank_free[6]])
                t = p.dve(lambda v, cs_=cs_: v.tensor_tensor(wbt[:, :], wTh[:, cs_],
                                                             P[6][:, 256:384], ALU.mult),
                          deps=[t, b["lw"]])
                bank_free[6] = t
                tut = p.dve(lambda v: v.tensor_tensor(uT[:, :], wbt[:, :], einv[:, :],
                                                      ALU.mult), deps=[tei])
                t = p.pe(lambda t_: t_.transpose(PT[:, 128:256], uT[:, :], idb[:, :]),
                         deps=[tut, t, ptfree["u"]])
                tus = p.dve(lambda v: v.tensor_copy(uS[:, :], PT[:, 128:256]), deps=[t])
                ptfree["u"] = tus
                t = p.pe(lambda t_, cs_=cs_: t_.matmul(P[0][:, 0:128], kTh[:, cs_],
                                                       qTh[:, cs_], start=True, stop=True),
                         deps=[b["lk"], b["tq"], tg])
                tab = p.dve(lambda v: v.tensor_tensor(abar[:, :], P[0][:, 0:128], mst[:, :],
                                                      ALU.mult), deps=[t])
                t = p.pe(lambda t_: t_.matmul(P[1][:, 0:128], abar[:, :], uS[:, :],
                                              start=True, stop=True), deps=[tab, tus, tl1])
                bank_free[1] = t
                tm4 = p.pe(lambda t_, cs_=cs_: t_.matmul(P[0][:, 0:128], qTh[:, cs_],
                                                         Skb[:, :], start=True, stop=True),
                           deps=[tab, st_upd])
                bank_free[0] = tm4
                t = p.dve(lambda v: v.tensor_tensor(t1s[:, :], e1s[:, :], P[0][:, 0:128],
                                                    ALU.mult), deps=[tm4, te1])
                bank_free[0] = t
                t2_ = p.dve(lambda v: v.tensor_tensor(t2s[:, :], e2s[:, :], P[1][:, 0:128],
                                                      ALU.mult), deps=[bank_free[1], te2])
                bank_free[1] = t2_
                t = p.dve(lambda v: v.tensor_tensor(sps[:, :], t1s[:, :], t2s[:, :], ALU.add))
                t = p.dve(lambda v: v.reduce_max(nmx[:, 0:1], sps[:, :], AX.X, negate=True))
                tpa = p.act(lambda s_: s_.activation(pa[:, :], sps[:, :], AF.Exp,
                                                     bias=nmx[:, 0:1], accum_out=zr[:, 0:1]),
                            deps=[t])
                t = p.dve(lambda v: v.reciprocal(rz[:, 0:1], zr[:, 0:1]), deps=[tpa])
                ta2 = p.dve(lambda v: v.scalar_tensor_tensor(a2[:, :], pa[:, :], rz[:, 0:1],
                                                             e2m[:, :], ALU.mult, ALU.mult),
                            deps=[te2m])
                t = p.pe(lambda t_: t_.transpose(PT[:, 256:384], a2[:, :], idb[:, :]),
                         deps=[ta2, tus, ptfree["a2"]])
                ta2t = p.dve(lambda v: v.tensor_copy(a2t[:, :], PT[:, 256:384]), deps=[t])
                ptfree["a2"] = ta2t
                trat = p.act(lambda s_: s_.activation(ratc[:, 0:1], lmT[:, 64:65], AF.Exp,
                                                      bias=cst[:, 4:5]), deps=[tl2])
                ta1t = p.dve(lambda v: v.tensor_scalar(a1t[:, :], a2t[:, :], ratc[:, 0:1],
                                                       None, ALU.mult), deps=[ta2t, trat])
                t = p.pe(lambda t_: t_.matmul(P[2][:, 0:128], uT[:, :], a2t[:, :],
                                              start=True, stop=True),
                         deps=[ta2t, tut, tl2, bank_free[2]])
                tpb = p.dve(lambda v: v.scalar_tensor_tensor(pbar[:, :], mst[:, :],
                                                             E44, P[2][:, 0:128], ALU.mult,
                                                             ALU.mult), deps=[t])
                bank_free[2] = tpb
                t = p.pe(lambda t_: t_.matmul(P[5][:, 0:128], pbar[:, :], Vc[:, :],
                                              start=True, stop=False),
                         deps=[tpb, lvc, bank_free[5]])
                to = p.pe(lambda t_: t_.matmul(P[5][:, 0:128], a1t[:, :], Svb[:, :],
                                               start=False, stop=True),
                          deps=[ta1t, st_upd])
                t = p.act(lambda s_: s_.activation(sqh[:, :], P[5][:, 0:128], AF.Square,
                                                   accum_out=ss1[:, 0:1]), deps=[to])
                t = p.dve(lambda v: v.tensor_scalar(ss1[:, 0:1], ss1[:, 0:1], 1.0 / 128.0,
                                                    None, ALU.mult), deps=[t])
                t = p.act(lambda s_: s_.activation(rr1[:, 0:1], ss1[:, 0:1], AF.Ln,
                                                   bias=cst[:, 0:1]), deps=[t])
                t = p.act(lambda s_: s_.activation(rr1[:, 0:1], rr1[:, 0:1], AF.Exp,
                                                   scale=-0.5))
                t = p.dve(lambda v: v.tensor_scalar(o1[:, :], P[5][:, 0:128], rr1[:, 0:1],
                                                    None, ALU.mult), deps=[t])
                bank_free[5] = t
                t = p.dve(lambda v: v.tensor_tensor(o2[:, :], o1[:, :], nwb[:, :], ALU.mult))
                t = p.dve(lambda v: v.tensor_tensor(og[:, :], o2[:, :], gatec[:, :],
                                                    ALU.mult), deps=[lgt])
                t = p.pe(lambda t_: t_.transpose(PT[:, 384:512], og[:, :], idb[:, :]),
                         deps=[t, ta2t, ptfree["og"]])
                otok = p.dve(lambda v, bh=bh, cs_=cs_: v.tensor_copy(
                    oT[bh][:, cs_], PT[:, 384:512]), deps=[t, glob["otok"]])
                glob["otok"] = otok
                ptfree["og"] = otok
                t = p.pe(lambda t_: t_.matmul(P[3][:, 0:128], Kc[:, :], uS[:, :],
                                              start=True, stop=True),
                         deps=[lkc, tus, tr, bank_free[3]])
                tdk = t
                t = p.pe(lambda t_: t_.matmul(P[4][:, 0:128], uS[:, :], Vc[:, :],
                                              start=True, stop=True),
                         deps=[tus, lvc, bank_free[4]])
                tdv = t
                t = p.dve(lambda v: v.tensor_tensor(trow[0:1, :], erow[0:1, :], mrow[0:1, :],
                                                    ALU.subtract), deps=[tr])
                t = p.dve(lambda v: v.tensor_scalar(trow[0:1, :], trow[0:1, :],
                                                    SHIFT, None, ALU.add), deps=[t])
                tlr = p.act(lambda s_: s_.activation(trow[0:1, :], trow[0:1, :], AF.Exp),
                            deps=[t])
                tle = p.act(lambda s_: s_.activation(erow[0:1, :], erow[0:1, :], AF.Exp),
                            deps=[t])
                t = p.pe(lambda t_: t_.matmul(P[6][:, 0:128], ones_r[0:1, 0:128], erow[0:1, :],
                                              start=True, stop=True),
                         deps=[tle, bank_free[6]])
                t = p.pe(lambda t_: t_.matmul(P[6][:, 128:256], ones_r[0:1, 0:128], trow[0:1, :],
                                              start=True, stop=True), deps=[tlr])
                t = p.dve(lambda v: v.tensor_copy(lend[:, :], P[6][:, 0:128]), deps=[t])
                t = p.dve(lambda v: v.tensor_copy(lrel[:, :], P[6][:, 128:256]))
                bank_free[6] = t
                tcolv = p.act(lambda s_: s_.activation(ecol[:, 0:1], lmT[:, 127:128],
                                                       AF.Exp), deps=[tl2])
                t = p.dve(lambda v: v.tensor_tensor(tcol[:, 0:1], lmT[:, 127:128],
                                                    lmT[:, 64:65], ALU.subtract), deps=[tl2])
                trc = p.act(lambda s_: s_.activation(rcol[:, 0:1], tcol[:, 0:1], AF.Exp,
                                                     bias=cst[:, 4:5]), deps=[t])
                t = p.dve(lambda v: v.tensor_tensor(tsk[:, :], P[3][:, 0:128], lrel[:, :],
                                                    ALU.mult), deps=[tdk, bank_free[6]])
                bank_free[3] = t
                t = p.dve(lambda v, cur=cur: v.tensor_tensor(tsk2[:, :], Sk[cur][:, :],
                                                             lend[:, :], ALU.mult))
                t = p.dve(lambda v, nxt=nxt: v.tensor_tensor(Sk[nxt][:, :], tsk[:, :],
                                                             tsk2[:, :], ALU.add))
                tskb = p.dve(lambda v, nxt=nxt: v.tensor_copy(Skb[:, :], Sk[nxt][:, :]),
                             deps=[tm4, t])
                t = p.dve(lambda v: v.tensor_scalar(tsv[:, :], P[4][:, 0:128], rcol[:, 0:1],
                                                    None, ALU.mult), deps=[tdv, trc])
                bank_free[4] = t
                t = p.dve(lambda v, cur=cur, nxt=nxt: v.scalar_tensor_tensor(
                    Sv[nxt][:, :], Sv[cur][:, :], ecol[:, 0:1], tsv[:, :],
                    ALU.mult, ALU.add), deps=[tcolv, to])
                tsvb = p.dve(lambda v, nxt=nxt: v.tensor_copy(Svb[:, :], Sv[nxt][:, :]),
                             deps=[t])
                b["st"] = tsvb
                bank_free[0] = tskb

            for bh in range(H):
                head_load(bodies[bh % 2], bh)
                for ch in range(32):
                    body(bodies[bh % 2], bh, ch)
            return glob["otok"]
    def ln2_gate(h3buf, ag_free, entry=None):
        with ExitStack() as ph:
            psb = lambda shp, d_: ph.enter_context(nc.sbuf_tensor(fresh("l2"), shp, d_))
            lf = [psb([128, D], F32) for _ in range(2)]
            lnb = psb([128, D], BF16)
            ln2w_f = psb([128, D], F32)
            ln2b_f = psb([128, D], F32)
            combT_sb = psb([4, TOK], F32)
            lw1 = p.dma("sp", ln2w_f[:, :], IN["ln2w"], deps=[entry])
            lw2 = p.dma("sp", ln2b_f[:, :], IN["ln2b"])
            ltb = [psb([128, 128], F32) for _ in range(2)]
            gt4 = psb([128, 4], F32)
            er = psb([128, 4], F32)
            rr4 = psb([128, 4], F32)
            sel4 = psb([128, 4], F32)
            cb4 = psb([128, 4], F32)
            c1 = psb([128, 1], F32)
            c2 = psb([128, 1], F32)
            c3 = psb([128, 1], F32)
            vt, at = [], []
            lwx = p.dma("sp", ln2w_f[:, :], IN["ln2w"], deps=[entry]) if False else None
            for i in range(NT):
                dep = [p.now("dve"), entry] if i == 0 else []
                vt.append(p.dve(lambda v, i=i: v.reduce_sum(st1[:, i:i + 1], h3buf[i][:, :], AX.X),
                                deps=dep))
                at.append(p.act(lambda s, i=i: s.activation(
                    sqd[:, :], h3buf[i][:, :], AF.Square, accum_out=st2[:, i:i + 1]), deps=dep))
            ok = []
            for i in range(NT):
                p.dve(lambda v, i=i: v.tensor_scalar(st1[:, i:i + 1], st1[:, i:i + 1],
                                                     1.0 / D, None, ALU.mult), deps=[vt[i]])
                p.dve(lambda v, i=i: v.tensor_scalar(st2[:, i:i + 1], st2[:, i:i + 1],
                                                     1.0 / D, None, ALU.mult), deps=[at[i]])
                p.dve(lambda v, i=i: v.scalar_tensor_tensor(
                    st2[:, i:i + 1], st1[:, i:i + 1], st1[:, i:i + 1], st2[:, i:i + 1],
                    ALU.mult, ALU.subtract))
                t = p.dve(lambda v, i=i: v.tensor_scalar(st2[:, i:i + 1], st2[:, i:i + 1],
                                                         -1.0, None, ALU.mult))
                t = p.act(lambda s, i=i: s.activation(st3[:, i:i + 1], st2[:, i:i + 1],
                                                      AF.Ln, bias=cst[:, 0:1]), deps=[t])
                t = p.act(lambda s, i=i: s.activation(rr_t[:, i:i + 1], st3[:, i:i + 1],
                                                      AF.Exp, scale=-0.5))
                t = p.dve(lambda v, i=i: v.tensor_scalar(
                    nmr_t[:, i:i + 1], st1[:, i:i + 1], rr_t[:, i:i + 1], -1.0,
                    ALU.mult, ALU.mult), deps=[t])
                ok.append(t)
            sto = ag_free
            combtok = None
            bank_free = {0: None, 1: None, 2: None, 3: None, 4: None}
            for i in range(NT):
                sl = i % 2
                t = p.dve(lambda v, i=i, sl=sl: v.tensor_scalar(
                    lf[sl][:, :], h3buf[i][:, :], rr_t[:, i:i + 1], nmr_t[:, i:i + 1],
                    ALU.mult, ALU.add), deps=[ok[i], bank_free.get("lf%d" % sl)])
                t = p.dve(lambda v, sl=sl: v.tensor_tensor(lf[sl][:, :], lf[sl][:, :],
                                                           ln2w_f[:, :], ALU.mult), deps=[lw2])
                tf = p.dve(lambda v, sl=sl: v.tensor_tensor(lf[sl][:, :], lf[sl][:, :],
                                                            ln2b_f[:, :], ALU.add))
                t = p.dve(lambda v, sl=sl: v.tensor_copy(lnb[:, :], lf[sl][:, :]), deps=[sto])
                sto = p.dma("sp", ag_in.ap()[128 * i:128 * (i + 1), :], lnb[:, :], deps=[t])
                gp = P[2 + (i % 2)]
                gslot = 2 + (i % 2)
                mmt = None
                for k in range(16):
                    ts_ = k % 2
                    tt_ = p.pe(lambda t_, sl=sl, k=k, ts_=ts_: t_.transpose(
                        P[ts_][:, 0:128], lf[sl][:, 128 * k:128 * (k + 1)], idf[:, :]),
                        deps=[tf, bank_free[ts_]])
                    tc = p.dve(lambda v, ts_=ts_: v.tensor_copy(ltb[ts_][:, :], P[ts_][:, 0:128]),
                               deps=[tt_])
                    bank_free[ts_] = tc
                    mmt = p.pe(lambda t_, k=k, ts_=ts_, gp=gp: t_.matmul(
                        gp[:, 0:4], ltb[ts_][:, :], wgate_s[:, 4 * k:4 * (k + 1)],
                        start=(k == 0), stop=(k == 15)), deps=[tc, bank_free[gslot]])
                tg4 = p.dve(lambda v, gp=gp: v.tensor_copy(gt4[:, :], gp[:, 0:4]), deps=[mmt])
                bank_free[gslot] = tg4
                t = p.dve(lambda v: v.reduce_max(c1[:, 0:1], gt4[:, :], AX.X, negate=True))
                t = p.act(lambda s_: s_.activation(er[:, :], gt4[:, :], AF.Exp,
                                                   bias=c1[:, 0:1], accum_out=c2[:, 0:1]),
                          deps=[t])
                t = p.dve(lambda v: v.reciprocal(c3[:, 0:1], c2[:, 0:1]), deps=[t])
                t = p.dve(lambda v: v.tensor_scalar(rr4[:, :], er[:, :], c3[:, 0:1], None,
                                                    ALU.mult))
                for e in range(4):
                    js = [j for j in range(4) if j != e]
                    t = p.dve(lambda v, e=e, j=js[0]: v.tensor_tensor(
                        sel4[:, e:e + 1], rr4[:, j:j + 1], rr4[:, e:e + 1], ALU.is_gt))
                    for j in js[1:]:
                        t = p.dve(lambda v, e=e, j=j: v.tensor_tensor(
                            c1[:, 0:1], rr4[:, j:j + 1], rr4[:, e:e + 1], ALU.is_gt))
                        t = p.dve(lambda v, e=e: v.tensor_tensor(
                            sel4[:, e:e + 1], sel4[:, e:e + 1], c1[:, 0:1], ALU.add))
                t = p.dve(lambda v: v.tensor_scalar(sel4[:, :], sel4[:, :], 1.5, None, ALU.is_lt))
                t = p.dve(lambda v: v.reduce_max(c1[:, 0:1], rr4[:, :], AX.X, negate=True),
                          deps=[t])
                t = p.act(lambda s_: s_.activation(er[:, :], rr4[:, :], AF.Exp,
                                                   bias=c1[:, 0:1]), deps=[t])
                t = p.dve(lambda v: v.tensor_tensor(er[:, :], er[:, :], sel4[:, :], ALU.mult),
                          deps=[t])
                t = p.dve(lambda v: v.reduce_sum(c2[:, 0:1], er[:, :], AX.X))
                t = p.dve(lambda v: v.reciprocal(c3[:, 0:1], c2[:, 0:1]))
                t = p.dve(lambda v: v.tensor_scalar(cb4[:, :], er[:, :], c3[:, 0:1], None,
                                                    ALU.mult))
                tt_ = p.pe(lambda t_: t_.transpose(P[4][0:4, 0:128], cb4[:, :], idf[:, :]),
                           deps=[t, bank_free[4]])
                combtok = p.dve(lambda v, i=i: v.tensor_copy(
                    combT_sb[0:4, 128 * i:128 * (i + 1)], P[4][0:4, 0:128]), deps=[tt_, combtok])
                bank_free[4] = combtok
                bank_free["lf%d" % sl] = tg4
            csto = p.dma("sp", combT_d.ap()[0:4, :], combT_sb[0:4, :], deps=[combtok])
            return sto, csto

    def moe(cc_ln2, cc_comb, rs_free, rsout_free, join):
        with ExitStack() as ph:
            psb = lambda shp, d_: ph.enter_context(nc.sbuf_tensor(fresh("mo"), shp, d_))
            cg = psb([16, TOK], F32)
            comb_row = psb([1, S], F32)
            lcg = p.dma("sp", cg[:, :], combT_g.ap(), deps=[cc_comb, join])
            crt = None
            for r in range(4):
                for n2 in range(2):
                    slot = (2 * r + n2) % 2
                    mt = p.pe(lambda t_, r=r, n2=n2, slot=slot: t_.matmul(
                        P[slot][0:1, :], onehot_s[:, r:r + 1],
                        cg[:, 512 * n2:512 * (n2 + 1)], start=True, stop=True),
                        deps=[lcg, crt])
                    crt = p.dve(lambda v, r=r, n2=n2, slot=slot: v.tensor_copy(
                        comb_row[0:1, TOK * r + 512 * n2:TOK * r + 512 * (n2 + 1)],
                        P[slot][0:1, :]), deps=[mt])
            with ExitStack() as ph1:
                psb1 = lambda shp, d_: ph1.enter_context(nc.sbuf_tensor(fresh("m1"), shp, d_))
                we1_sb = psb1([128, 16 * D], BF16)
                rhsT = psb1([128, 8192], BF16)
                cbc = psb1([128, 512], BF16)
                hm1 = [psb1([128, 512], BF16) for _ in range(2)]
                hm2 = [psb1([128, 512], BF16) for _ in range(2)]
                wt = cast_w(we1_sb, IN["we1e"], D, nk=16, nsplit=8, deps=[join, cc_ln2])
                prev = [None, None]
                sto = join
                cbtok = None
                for r in range(4):
                    for hh in range(2):
                        rt = load_rhsT(rhsT, r, hh, cc_ln2, extra=[sto])
                        c0 = TOK * r + 512 * hh
                        mt = p.pe(lambda t_, c0=c0: t_.matmul(
                            P[2][:, 0:512], ones_r[0:1, 0:128], comb_row[0:1, c0:c0 + 512],
                            start=True, stop=True), deps=[crt, cbtok])
                        tc = p.dve(lambda v: v.tensor_copy(cbc[:, :], P[2][:, 0:512]), deps=[mt])
                        cbtok = tc
                        for It in range(16):
                            slot = It % 2
                            bank = P[slot]

                            def mmf(t, It=It, bank=bank):
                                ins = None
                                for k in range(16):
                                    ins = t.matmul(bank[:, :],
                                                   we1_sb[:, D * k + 128 * It:D * k + 128 * (It + 1)],
                                                   rhsT[:, 512 * k:512 * (k + 1)],
                                                   start=(k == 0), stop=(k == 15))
                                return ins
                            mt2 = p.pe(mmf, deps=[rt, wt, prev[slot]])
                            t = p.act(lambda s_, It=It, bank=bank, slot=slot: s_.activation(
                                hm1[slot][:, :], bank[:, :], AF.Gelu, bias=be1_s[:, It:It + 1]),
                                deps=[mt2])
                            prev[slot] = t
                            t = p.dve(lambda v, slot=slot: v.tensor_tensor(
                                hm2[slot][:, :], hm1[slot][:, :], cbc[:, :], ALU.mult),
                                deps=[t, tc, sto])
                            sto = p.dma("sp",
                                        hm_d.ap()[128 * It:128 * (It + 1), c0:c0 + 512],
                                        hm2[slot][:, :], deps=[t])
            with ExitStack() as ph2:
                psb2 = lambda shp, d_: ph2.enter_context(nc.sbuf_tensor(fresh("m2"), shp, d_))
                we2_sb = psb2([128, 16 * D], BF16)
                ht = psb2([128, 16 * 128], BF16)
                be2r = psb2([1, D], F32)
                lb2 = p.dma("sp", be2r[:, :], IN["be2e"])
                stg = [psb2([128, 512], F32) for _ in range(2)]
                wt2 = cast_w(we2_sb, IN["we2e"], D, nk=16, nsplit=8, deps=[sto])
                prev = [None, None]
                sto2 = rs_free
                lh = sto
                for tt in range(32):
                    lh = p.dma("sp", ht[:, :].rearrange("p (k j) -> p k j", j=128),
                               hm_d.ap()[:, 128 * tt:128 * (tt + 1)].rearrange(
                                   "(k p) j -> p k j", p=128),
                               deps=[sto, prev[0], prev[1]])
                    for n in range(4):
                        slot = n % 2
                        bank = P[slot]

                        def mmf2(t, tt=tt, n=n, bank=bank):
                            for k in range(16):
                                t.matmul(bank[:, :], ht[:, 128 * k:128 * (k + 1)],
                                         we2_sb[:, D * k + 512 * n:D * k + 512 * (n + 1)],
                                         start=(k == 0), stop=False)
                            return t.matmul(bank[:, :],
                                            comb_row[0:1, 128 * tt:128 * (tt + 1)],
                                            be2r[0:1, 512 * n:512 * (n + 1)],
                                            start=False, stop=True)
                        mt = p.pe(mmf2, deps=[lh, wt2, lb2, crt, prev[slot]])
                        sg = stg[slot]
                        t = p.dve(lambda v, bank=bank, sg=sg: v.tensor_copy(sg[:, :], bank[:, :]),
                                  deps=[mt, sto2])
                        prev[slot] = t
                        sto2 = p.dma("sp",
                                     rs_in.ap()[128 * tt:128 * (tt + 1), 512 * n:512 * (n + 1)],
                                     sg[:, :], deps=[t])
                cc = p.coll("ReduceScatter", ALU.add, [rs_in.ap().opt()], [rs_out.ap().opt()],
                            deps=[sto2, rsout_free])
            return cc

    state = {"x_src": x_in, "ag_free": SET, "agout_free": SET, "entry": SET,
             "rs_ccs": [SET] * 4, "rsout_free_c": [SET] * 4,
             "rs_free_moe": SET, "rsout_free_moe": SET}

    def sub_attn_a(final):
        lnsto, _ = ln_phase(state["x_src"], "plain", state["ag_free"], entry=state["entry"])
        cc = p.coll("AllGather", ALU.bypass, [ag_in.ap().opt()], [ag_out.ap().opt()],
                    deps=[lnsto, state["agout_free"]])
        join = p.now("sp")
        projdone = proj_a(cc, join)
        convdone = conv_phase(projdone)
        with ExitStack() as ph:
            oTbig = ph.enter_context(nc.sbuf_tensor(fresh("oTa"), [128, 4 * S], BF16))
            oT = [oTbig[:, S * h:S * (h + 1)] for h in range(4)]
            otok = scan_phase(convdone, oT)
            if "oT_dbg" in _DEBUG:
                for bh in range(4):
                    otok = p.dma("sp", oT_dbg.ap()[128 * bh:128 * (bh + 1), :], oT[bh][:, :],
                                 deps=[otok])
            cc2 = out_proj_rs(oTbig, "woa_s", otok, state["rs_ccs"], state["rsout_free_c"])
            last, cfree = residual(state["x_src"], cc2, "plain", final=final)
        state["x_src"] = x_dram.ap()
        state["ag_free"] = cc
        state["agout_free"] = p.now("tq")
        state["rs_ccs"] = cc2
        state["rsout_free_c"] = cfree
        state["entry"] = last
        return last

    def sub_softmax_core(wqn, wkn, wvn, bqn, bkn, bvn, won, causal, lnmode):
        lnsto, _ = ln_phase(state["x_src"], lnmode, state["ag_free"], entry=state["entry"])
        cc = p.coll("AllGather", ALU.bypass, [ag_in.ap().opt()], [ag_out.ap().opt()],
                    deps=[lnsto, state["agout_free"]])
        seed = p.now("sp")
        projdone = proj_qkv(IN[wqn], IN[wkn], IN[wvn], bqn, bkn, bvn, cc, seed)
        with ExitStack() as ph:
            oTbig = ph.enter_context(nc.sbuf_tensor(fresh("oTs"), [128, 4 * S], BF16))
            oT = [oTbig[:, S * h:S * (h + 1)] for h in range(4)]
            otok = attn_inner(causal, oT, projdone)
            if "oT_dbg" in _DEBUG:
                for bh in range(4):
                    otok = p.dma("sp", oT_dbg.ap()[128 * bh:128 * (bh + 1), :], oT[bh][:, :],
                                 deps=[otok])
            cc2 = out_proj_rs(oTbig, won, otok, state["rs_ccs"], state["rsout_free_c"])
        state["x_src"] = x_dram.ap()
        state["ag_free"] = cc
        state["agout_free"] = p.now("tq")
        state["rs_ccs"] = cc2
        return cc2

    def sub_attn_b(final):
        xsrc = state["x_src"]
        cc2 = sub_softmax_core("wqb", "wkb", "wvb", "bqb", "bkb", "bvb",
                               "wob_s", False, "plain")
        last, cfree = residual(xsrc, cc2, "plain", final=final, bias=rbias["obb_bc"])
        state["rsout_free_c"] = cfree
        state["entry"] = last
        return last

    def sub_ffn(final):
        xsrc = state["x_src"]
        cc2 = sub_softmax_core("wqf", "wkf", "wvf", "bqf", "bkf", "bvf",
                               "wof_s", True, "ffn")
        with ExitStack() as ph:
            h3buf = [ph.enter_context(nc.sbuf_tensor(fresh("h3"), [128, D], F32))
                     for _ in range(NT)]
            rtok, cfree = residual(xsrc, cc2, "ffn", h3buf=h3buf, bias=rbias["bof_bc"])
            state["rsout_free_c"] = cfree
            ln2sto, csto = ln2_gate(h3buf, state["ag_free"], entry=rtok)
        cc3 = p.coll("AllGather", ALU.bypass, [ag_in.ap().opt()], [ag_out.ap().opt()],
                     deps=[ln2sto, state["agout_free"]])
        cc4 = p.coll("AllGather", ALU.bypass, [combT_d.ap().opt()], [combT_g.ap().opt()],
                     deps=[csto])
        cc5 = moe(cc3, cc4, state["rs_free_moe"], state["rsout_free_moe"], p.now("sp"))
        last, mfree = residual(x_dram.ap(), cc5, "plain", final=final)
        state["ag_free"] = cc3
        state["agout_free"] = p.now("tq")
        state["rs_free_moe"] = cc5
        state["rsout_free_moe"] = mfree
        state["entry"] = last
        return last

    subs = [sub_attn_a, sub_ffn, sub_attn_b, sub_ffn]
    last = None
    for si in range(_TRUNC):
        last = subs[si](si == _TRUNC - 1)

    dbg_map = {"ag_in": ag_in, "ag_out": ag_out, "qT_d": qT_d, "kT_d": kT_d,
               "vT_d": vT_d, "wT_d": wT_d, "zT_d": zT_d, "v_tm": v_tm,
               "qcT_d": qcT_d, "kcT_d": kcT_d, "vcT_d": vcT_d, "wcT_d": wcT_d,
               "g_d": g_d, "beta_d": beta_d, "gate_d": gate_d, "hm_d": hm_d,
               "rs_in": rs_in, "rs_out": rs_out, "x_dram": x_dram,
               "combT_g": combT_g, "oT_dbg": oT_dbg, "sc_dbg": sc_dbg}
    for nm in _DEBUG:
        th = dbg_map[nm]
        shp = list(th.ap().tensor.shape)
        od = dt("dbg_" + nm, shp, th.ap().tensor.dtype, kind="ExternalOutput")
        last = p.dma("sp", od.ap(), th.ap(), deps=[last])

    p.op("sp", lambda e: e.nop(), deps=[last], inc=0)

    p.emit(block)
    blockstk.close()
    stk.close()
    p.stk.close()
    return nc


def kernel(**inputs):
    per_core = _prep(inputs)
    nc = build()
    res = run_bass_kernel_spmd(nc, per_core, core_ids=list(range(NC)))
    outs = [res.results[c]["y"] for c in range(NC)]
    if _DEBUG:
        kernel.dbg = res.results
    return np.concatenate(outs, axis=0).reshape(2, 4096, D)
